# version 01
# baseline (speedup 1.0000x reference)
import numpy as np
import ml_dtypes
import concourse.bass as bass
import concourse.mybir as mybir
from contextlib import ExitStack

F32 = mybir.dt.float32
BF16 = mybir.dt.bfloat16
AF = mybir.ActivationFunctionType
ALU = mybir.AluOpType
AX = mybir.AxisListType
AP = bass.AP

D = 2048
DT = 16
H = 16
SF = 4096
NHC = 16
T = 2048
NQC = 4
HID = 8192
HTN = 64
NG = 32
NDC = 8
SCALE = float(1.0 / np.sqrt(128.0))
EPS = 1e-5
RMEAN = float(1.0 / 2048.0)
RSQ = float(1.0 / np.sqrt(2048.0))

ENGS = ("tensor", "scalar", "vector", "sync", "gpsimd")


class Rec:
    def __init__(self):
        self.ops = {k: [] for k in ENGS}
        self.cnt = {}

    def w(self, eng, sem, val):
        if val > 0:
            self.ops[eng].append(("w", sem, val))

    def do(self, eng, fn, sem=None, amt=1):
        if sem is not None:
            self.cnt[sem] = self.cnt.get(sem, 0) + amt
        self.ops[eng].append(("c", fn, sem, amt))

    def c(self, sem):
        return self.cnt.get(sem, 0)

    def replay(self, eng, e, sems):
        for item in self.ops[eng]:
            if item[0] == "w":
                e.wait_ge(sems[item[1]], item[2])
            else:
                _, fn, sem, amt = item
                ins = fn(e)
                if sem is not None:
                    ins.then_inc(sems[sem], amt)


SEM_NAMES = [
    "d_wkv", "d_wq", "d_xt0", "d_xt1", "d_wo0", "d_wo1", "d_wg0", "d_wg1",
    "d_wd0", "d_wd1", "d_id", "d_xn0", "d_xn1", "d_y1f0", "d_y1f1",
    "d_y1b0", "d_y1b1", "d_y1T", "d_gw0", "d_gw1", "d_gr0", "d_gr1",
    "d_y1c0", "d_y1c1", "d_r2w0", "d_r2w1", "d_r2r0", "d_r2r1", "d_out",
    "s_init", "s_Ak", "s_Akf", "s_Av", "s_Avt", "s_Atr", "s_Avn",
    "s_Aq", "s_Aqf",
    "s_st", "s_exp", "s_ho", "s_sums", "s_recip", "s_rep", "s_repc", "s_hoevac",
    "s_att", "s_r1v", "s_stat", "s_nvpe", "s_std", "s_y1f", "s_y1b",
    "s_gate", "s_up", "s_silu", "s_gated", "sD1_g",
    "s_down", "s_r2v", "s_stat2", "s_nvpe2", "s_std2", "s_y2",
]


def build(upto="full", debug=False):
    nc = bass.Bass("TRN2", target_bir_lowering=False)

    xt_h = nc.dram_tensor("xt_h", [128, NHC * DT * 256], F32, kind="ExternalInput")
    xn_h = nc.dram_tensor("xn_h", [T, D], F32, kind="ExternalInput")
    wq_h = nc.dram_tensor("wq_h", [128, DT * D], F32, kind="ExternalInput")
    wk_h = nc.dram_tensor("wk_h", [128, DT * 128], F32, kind="ExternalInput")
    wv_h = nc.dram_tensor("wv_h", [128, DT * 128], F32, kind="ExternalInput")
    wo_h = nc.dram_tensor("wo_h", [128, 8 * 16 * 256], F32, kind="ExternalInput")
    wg_h = nc.dram_tensor("wg_h", [128, NG * DT * 256], F32, kind="ExternalInput")
    wu_h = nc.dram_tensor("wu_h", [128, NG * DT * 256], F32, kind="ExternalInput")
    wd_h = nc.dram_tensor("wd_h", [128, NDC * HTN * 256], F32, kind="ExternalInput")
    id_h = nc.dram_tensor("id_h", [128, 128], BF16, kind="ExternalInput")

    skind = "ExternalOutput" if debug else "Internal"
    y1f_hbm = nc.dram_tensor("y1f_hbm", [T, D], F32, kind=skind)
    y1b_hbm = nc.dram_tensor("y1b_hbm", [T, D], BF16, kind=skind)
    gat_hbm = nc.dram_tensor("gat_hbm", [8 * HTN, 128, 256], BF16, kind=skind)
    r2_hbm = nc.dram_tensor("r2_hbm", [T, 7 * 256], F32, kind="Internal")
    y_h = nc.dram_tensor("y_h", [T, D], F32, kind="ExternalOutput")

    dbg = {}
    if debug:
        for nm, shp, dt in [("kT", [128, SF], BF16), ("vN", [128, SF], BF16),
                            ("qT", [128, H * T], BF16), ("ho", [128, 8192], BF16),
                            ("r1", [512, D], F32)]:
            dbg[nm] = nc.dram_tensor("dbg_" + nm, shp, dt, kind="ExternalOutput")

    with ExitStack() as ctx:
        E = ctx.enter_context
        RA = E(nc.sbuf_tensor("RA", [128, 32768], BF16))
        RB = E(nc.sbuf_tensor("RB", [128, 32768], BF16))
        RC = E(nc.sbuf_tensor("RC", [128, 8192], BF16))
        RD = E(nc.sbuf_tensor("RD", [128, 16384], BF16))
        RE = E(nc.sbuf_tensor("RE", [128, 8192], BF16))
        PTB = E(nc.sbuf_tensor("PTB", [128, 2048], BF16))
        Y1B = E(nc.sbuf_tensor("Y1B", [128, 4096], BF16))
        IDN = E(nc.sbuf_tensor("IDN", [128, 128], BF16))
        ONE = E(nc.sbuf_tensor("ONE", [128, 1], BF16))
        ONF = E(nc.sbuf_tensor("ONF", [1, 128], F32))
        LNS = E(nc.sbuf_tensor("LNS", [128, 24], F32))
        JK = E(nc.sbuf_tensor("JK", [128, 256], F32))

        stA = E(nc.psum_tensor("stA", [128, 1024], F32))
        stB = E(nc.psum_tensor("stB", [128, 1024], F32))
        ho0 = E(nc.psum_tensor("ho0", [128, 512], F32))
        ho1 = E(nc.psum_tensor("ho1", [128, 512], F32))
        sr0 = E(nc.psum_tensor("sr0", [128, 512], F32))
        sr1 = E(nc.psum_tensor("sr1", [128, 512], F32))

        sems = {n: E(nc.semaphore(n)) for n in SEM_NAMES}

        r = Rec()
        _emit(r, upto, debug, dict(
            xt_h=xt_h, xn_h=xn_h, wq_h=wq_h, wk_h=wk_h, wv_h=wv_h, wo_h=wo_h,
            wg_h=wg_h, wu_h=wu_h, wd_h=wd_h, id_h=id_h,
            y1f_hbm=y1f_hbm, y1b_hbm=y1b_hbm, gat_hbm=gat_hbm, r2_hbm=r2_hbm,
            y_h=y_h, dbg=dbg,
            RA=RA, RB=RB, RC=RC, RD=RD, RE=RE, PTB=PTB, Y1B=Y1B, IDN=IDN,
            ONE=ONE, ONF=ONF, LNS=LNS, JK=JK,
            stA=stA, stB=stB, ho0=ho0, ho1=ho1, sr0=sr0, sr1=sr1))

        block = E(nc.Block())

        @block.tensor
        def _(e):
            r.replay("tensor", e, sems)

        @block.scalar
        def _(e):
            r.replay("scalar", e, sems)

        @block.vector
        def _(e):
            r.replay("vector", e, sems)

        @block.sync
        def _(e):
            r.replay("sync", e, sems)

        @block.gpsimd
        def _(e):
            r.replay("gpsimd", e, sems)

    return nc


def _emit(r, upto, debug, t):
    xt_h, xn_h = t["xt_h"], t["xn_h"]
    wq_h, wk_h, wv_h, wo_h = t["wq_h"], t["wk_h"], t["wv_h"], t["wo_h"]
    wg_h, wu_h, wd_h, id_h = t["wg_h"], t["wu_h"], t["wd_h"], t["id_h"]
    y1f_hbm, y1b_hbm = t["y1f_hbm"], t["y1b_hbm"]
    gat_hbm, r2_hbm, y_h = t["gat_hbm"], t["r2_hbm"], t["y_h"]
    dbg = t["dbg"]
    RA, RB, RC, RD, RE = t["RA"], t["RB"], t["RC"], t["RD"], t["RE"]
    PTB, Y1B, IDN, ONE, ONF = t["PTB"], t["Y1B"], t["IDN"], t["ONE"], t["ONF"]
    LNS, JK = t["LNS"], t["JK"]
    stA, stB = t["stA"], t["stB"]
    ho0, ho1, sr0, sr1 = t["ho0"], t["ho1"], t["sr0"], t["sr1"]

    RAf = RA[:, :].bitcast(F32)
    RCf = RC[:, :].bitcast(F32)
    RDf = RD[:, :].bitcast(F32)
    REf = RE[:, :].bitcast(F32)

    def kT(a, b):
        return RE[:, a:b]

    def vN(a, b):
        return RE[:, 4096 + a:4096 + b]

    def r1_ap(qt, a, b):
        return RAf[:, qt * 2048 + a: qt * 2048 + b]

    def y1f_ap(par):
        return RAf[:, 8192 + par * 2048: 8192 + (par + 1) * 2048]

    def rep_ap(par):
        return RAf[:, 12288 + par * 512: 12288 + (par + 1) * 512]

    def xnt_ap(par):
        return RAf[:, 13312 + par * 256: 13312 + (par + 1) * 256]

    def rcp_ap(par):
        return RAf[0:1, 13824 + par * 512: 13824 + (par + 1) * 512]

    def xt_ap(par, a, b):
        return RC[:, par * 4096 + a: par * 4096 + b]

    def wo_ap(par, a, b):
        return RC[:, par * 4096 + a: par * 4096 + b]

    wk_sb = RD[:, 0:2048]
    wv_sb = RD[:, 2048:4096]

    def vt_ap(par):
        return RD[:, 4096 + par * 256: 4096 + (par + 1) * 256]

    def hoT_ap(par, a, b):
        return RD[:, par * 8192 + a: par * 8192 + b]

    def wg_ap(par, a, b):
        return RD[:, par * 4096 + a: par * 4096 + b]

    def wu_ap(par, a, b):
        return RC[:, par * 4096 + a: par * 4096 + b]

    def gstage_ap(j, a, b):
        return RE[:, j * 2048 + a: j * 2048 + b]

    def silu_ap(par):
        return REf[:, 2048 + par * 512: 2048 + (par + 1) * 512]

    def y1c_ap(par):
        return REf[:, par * 256: (par + 1) * 256]

    def r2c_ap(par):
        return REf[:, 512 + par * 256: 512 + (par + 1) * 256]

    def r2f_ap(par, a, b):
        return RCf[:, par * 2048 + a: par * 2048 + b]

    def y2_ap(par):
        return RDf[:, par * 2048: (par + 1) * 2048]

    def gat_sb(par, a, b):
        return RA[:, par * 16384 + a: par * 16384 + b]

    def wd_sb(par, a, b):
        return RB[:, par * 16384 + a: par * 16384 + b]

    mI4 = lambda i: LNS[:, i:i + 1]
    e24 = lambda i: LNS[:, 4 + i:5 + i]
    nv4 = lambda i: LNS[:, 8 + i:9 + i]
    sd4 = lambda i: LNS[:, 12 + i:13 + i]
    rs4 = lambda i: LNS[:, 16 + i:17 + i]

    r.do("gpsimd", lambda e: e.dma_start(wk_sb, wk_h[:, :]), "d_wkv", 16)
    r.do("gpsimd", lambda e: e.dma_start(wv_sb, wv_h[:, :]), "d_wkv", 16)
    r.do("gpsimd", lambda e: e.dma_start(RA[:, :], wq_h[:, :]), "d_wq", 16)
    r.do("sync", lambda e: e.dma_start(IDN[:, :], id_h[:, :]), "d_id", 16)
    r.do("vector", lambda e: e.memset(ONE[:, :], 1.0))
    r.do("vector", lambda e: e.memset(LNS[:, 20:21], EPS))
    r.do("vector", lambda e: e.memset(ONF[:, :], 1.0), "s_init", 1)

    for c in range(NHC):
        par = c % 2
        dsem = "d_xt%d" % par
        if c >= 2:
            cp = c - 2
            if cp < 8:
                r.w("gpsimd", "s_Av", cp + 1)
            else:
                r.w("gpsimd", "s_Aq", (cp - 8 + 1) * DT)
        r.do("gpsimd",
             (lambda c=c, par=par: (lambda e: e.dma_start(
                 xt_ap(par, 0, 4096), xt_h[:, c * 4096:(c + 1) * 4096])))(),
             dsem, 16)

        r.w("tensor", dsem, 16 * (c // 2 + 1))
        if c == 0:
            r.w("tensor", "d_wkv", 32)
            r.w("tensor", "d_id", 16)
        if c >= 1:
            r.w("tensor", "s_Akf", r.c("s_Akf"))
        for i in range(DT):
            r.do("tensor",
                 (lambda i=i, par=par: (lambda e: e.matmul(
                     stA[:, 512:768], wk_sb[:, i * 128:(i + 1) * 128],
                     xt_ap(par, i * 256, (i + 1) * 256),
                     start=(i == 0), stop=(i == DT - 1))))(),
                 "s_Ak" if i == DT - 1 else None)
        if c >= 1:
            r.w("tensor", "s_Avt", r.c("s_Avt"))
        for i in range(DT):
            r.do("tensor",
                 (lambda i=i, par=par: (lambda e: e.matmul(
                     stB[:, 512:768], wv_sb[:, i * 128:(i + 1) * 128],
                     xt_ap(par, i * 256, (i + 1) * 256),
                     start=(i == 0), stop=(i == DT - 1))))(),
                 "s_Av" if i == DT - 1 else None)
        r.w("vector", "s_Ak", r.c("s_Ak"))
        r.do("vector",
             (lambda c=c: (lambda e: e.tensor_copy(
                 kT(c * 256, (c + 1) * 256), stA[:, 512:768])))(),
             "s_Akf", 1)
        r.w("vector", "s_Av", r.c("s_Av"))
        if c >= 2:
            r.w("vector", "s_Atr", c - 1)
        r.do("vector",
             (lambda par=par: (lambda e: e.tensor_copy(vt_ap(par), stB[:, 512:768])))(),
             "s_Avt", 1)
        r.w("tensor", "s_Avt", r.c("s_Avt"))
        if c >= 2:
            r.w("tensor", "s_Avn", r.c("s_Avn") - 1)
        hob = (ho0, ho1)[par]
        for j in range(2):
            r.do("tensor",
                 (lambda par=par, j=j, hob=hob: (lambda e: e.transpose(
                     hob[:, :].bitcast(BF16)[:, j * 128:(j + 1) * 128],
                     vt_ap(par)[:, j * 128:(j + 1) * 128], IDN[:, :])))(),
                 "s_Atr" if j == 1 else None)
        r.w("vector", "s_Atr", r.c("s_Atr"))
        r.do("vector",
             (lambda c=c, hob=hob: (lambda e: e.tensor_copy(
                 vN(c * 256, (c + 1) * 256), hob[:, :].bitcast(BF16)[:, 0:256])))(),
             "s_Avn", 1)
        if c >= 8:
            if c == 8:
                r.w("tensor", "d_wq", 16)
            for f in range(DT):
                gq = (c - 8) * DT + f
                qb = (stA, stB)[gq % 2]
                r.w("tensor", "s_Aqf", gq - 1)
                for i in range(DT):
                    r.do("tensor",
                         (lambda i=i, f=f, par=par, qb=qb: (lambda e: e.matmul(
                             qb[:, 0:256],
                             RA[:, i * 2048 + f * 128: i * 2048 + (f + 1) * 128],
                             xt_ap(par, i * 256, (i + 1) * 256),
                             start=(i == 0), stop=(i == DT - 1))))(),
                         "s_Aq" if i == DT - 1 else None)
                r.w("vector", "s_Aq", r.c("s_Aq"))
                r.do("vector",
                     (lambda c=c, f=f, qb=qb: (lambda e: e.tensor_copy(
                         RB[:, f * 2048 + (c - 8) * 256: f * 2048 + (c - 7) * 256],
                         qb[:, 0:256])))(),
                     "s_Aqf", 1)

    if upto == "A":
        r.w("sync", "s_Akf", NHC)
        r.w("sync", "s_Avn", NHC)
        r.w("sync", "s_Aqf", 128)
        r.do("sync", lambda e: e.dma_start(dbg["kT"][:, :], kT(0, 4096)), "d_out", 16)
        r.do("sync", lambda e: e.dma_start(dbg["vN"][:, :], vN(0, 4096)), "d_out", 16)
        r.do("sync", lambda e: e.dma_start(dbg["qT"][:, :], RB[:, :]), "d_out", 16)
        r.w("sync", "d_out", 48)
        return

    r.w("tensor", "s_Akf", NHC)
    r.w("tensor", "s_Avn", NHC)
    r.w("tensor", "s_Aqf", 128)
    r.w("tensor", "s_init", 1)

    def emit_rep(idx):
        srb = (sr0, sr1)[idx % 2]
        qc_, h_ = idx // 16, idx % 16
        r.w("tensor", "s_recip", idx + 1)
        r.do("tensor",
             (lambda idx=idx, srb=srb: (lambda e: e.matmul(
                 srb[:, :], ONF[:, :],
                 rcp_ap(idx % 2),
                 start=True, stop=True)))(),
             "s_rep", 1)
        r.w("scalar", "s_rep", r.c("s_rep"))
        r.w("scalar", "s_hoevac", idx - 1)
        r.do("scalar",
             (lambda idx=idx, srb=srb: (lambda e: e.copy(rep_ap(idx % 2), srb[:, :])))(),
             "s_repc", 1)
        r.w("vector", "s_repc", r.c("s_repc"))
        r.w("vector", "s_ho", idx + 1)
        if qc_ >= 2:
            r.w("vector", "s_att", 32 * (qc_ - 1))
        r.do("vector",
             (lambda idx=idx, qc_=qc_, h_=h_: (lambda e: e.tensor_tensor(
                 hoT_ap(qc_ % 2, h_ * 512, (h_ + 1) * 512),
                 (ho0, ho1)[idx % 2][:, :], rep_ap(idx % 2), ALU.mult)))(),
             "s_hoevac", 1)

    def emit_pv_pair(idx, p):
        gp = idx * 16 + p
        r.w("tensor", "s_exp", gp + 1)
        hob = (ho0, ho1)[idx % 2]
        srb = (sr0, sr1)[idx % 2]
        for sub in range(2):
            tt = 2 * p + sub
            if tt == 0 and idx >= 2:
                r.w("tensor", "s_hoevac", idx - 1)
                r.w("tensor", "s_repc", idx - 1)
            r.do("tensor",
                 (lambda tt=tt, p=p, sub=sub, hob=hob: (lambda e: e.matmul(
                     hob[:, :], vN(tt * 128, (tt + 1) * 128),
                     PTB[:, (p % 2) * 1024 + sub * 512:(p % 2) * 1024 + (sub + 1) * 512],
                     start=(tt == 0), stop=(tt == 31))))(),
                 "s_ho" if tt == 31 else None)
            r.do("tensor",
                 (lambda tt=tt, p=p, sub=sub, srb=srb: (lambda e: e.matmul(
                     srb[0:1, :], ONE[:, :],
                     PTB[:, (p % 2) * 1024 + sub * 512:(p % 2) * 1024 + (sub + 1) * 512],
                     start=(tt == 0), stop=(tt == 31))))(),
                 "s_sums" if tt == 31 else None)

    for qc in range(NQC):
        for dc in range(8):
            for qt in range(4):
                gxn = qc * 32 + dc * 4 + qt
                if gxn < 2:
                    r.w("sync", "s_Aq", 128)
                r.w("sync", "s_r1v", gxn - 1)
                r.do("sync",
                     (lambda qc=qc, dc=dc, qt=qt, gxn=gxn: (lambda e: e.dma_start(
                         xnt_ap(gxn % 2),
                         xn_h[qc * 512 + qt * 128: qc * 512 + (qt + 1) * 128,
                              dc * 256:(dc + 1) * 256])))(),
                     "d_xn%d" % (gxn % 2), 16)
        if qc == 0:
            r.w("gpsimd", "s_Aq", 128)
        for dc in range(8):
            gw = qc * 8 + dc
            if gw >= 2:
                gp_ = gw - 2
                r.w("gpsimd", "s_att", (gp_ // 8) * 32 + ((gp_ % 8) + 1) * 4)
            r.do("gpsimd",
                 (lambda dc=dc, gw=gw: (lambda e: e.dma_start(
                     wo_ap(gw % 2, 0, 4096), wo_h[:, dc * 4096:(dc + 1) * 4096])))(),
                 "d_wo%d" % (gw % 2), 16)

        if qc >= 1:
            r.w("tensor", "s_r1v", 32 * qc)
        for h in range(H):
            idx = qc * 16 + h
            for p in range(16):
                gp = idx * 16 + p
                stb = (stA, stB)[p % 2]
                r.w("tensor", "s_exp", gp - 1)
                for sub in range(2):
                    tt = 2 * p + sub
                    r.do("tensor",
                         (lambda tt=tt, h=h, qc=qc, stb=stb, sub=sub: (lambda e: e.matmul(
                             stb[:, sub * 512:(sub + 1) * 512],
                             kT(tt * 128, (tt + 1) * 128),
                             RB[:, h * 2048 + qc * 512: h * 2048 + (qc + 1) * 512],
                             start=True, stop=True)))(),
                         "s_st" if sub == 1 else None)
                if p == 1 and idx >= 1 and (idx % 16) != 0:
                    emit_rep(idx - 1)
                if p >= 1:
                    emit_pv_pair(idx, p - 1)
            emit_pv_pair(idx, 15)
            for p in range(16):
                gp = idx * 16 + p
                stb = (stA, stB)[p % 2]
                r.w("scalar", "s_st", gp + 1)
                r.do("scalar",
                     (lambda p=p, stb=stb: (lambda e: e.activation(
                         PTB[:, (p % 2) * 1024:(p % 2 + 1) * 1024], stb[:, :],
                         AF.Exp, scale=SCALE)))(),
                     "s_exp", 1)
            r.w("vector", "s_sums", idx + 1)
            r.w("vector", "s_rep", idx - 1)
            r.do("vector",
                 (lambda idx=idx: (lambda e: e.reciprocal(
                     rcp_ap(idx % 2),
                     (sr0, sr1)[idx % 2][0:1, :])))(),
                 "s_recip", 1)

        emit_rep(qc * 16 + 15)
        r.w("tensor", "s_exp", (qc + 1) * 256)
        r.w("tensor", "s_hoevac", (qc + 1) * 16)
        for dc in range(8):
            gw = qc * 8 + dc
            r.w("tensor", "d_wo%d" % (gw % 2), 16 * (gw // 2 + 1))
            for qt in range(4):
                g_att = qc * 32 + dc * 4 + qt
                ab = (stA, stB)[g_att % 2]
                r.w("tensor", "s_r1v", g_att - 1)
                for f in range(16):
                    r.do("tensor",
                         (lambda f=f, qt=qt, qc=qc, gw=gw, ab=ab: (lambda e: e.matmul(
                             ab[:, 0:256],
                             hoT_ap(qc % 2, f * 512 + qt * 128, f * 512 + (qt + 1) * 128),
                             wo_ap(gw % 2, f * 256, (f + 1) * 256),
                             start=(f == 0), stop=(f == 15))))(),
                         "s_att" if f == 15 else None)
                r.w("vector", "s_att", g_att + 1)
                r.w("vector", "d_xn%d" % (g_att % 2), 16 * (g_att // 2 + 1))
                r.do("vector",
                     (lambda qt=qt, dc=dc, g_att=g_att, ab=ab: (lambda e: e.tensor_tensor(
                         r1_ap(qt, dc * 256, (dc + 1) * 256), ab[:, 0:256],
                         xnt_ap(g_att % 2), ALU.add)))(),
                     "s_r1v", 1)
        if debug and upto == "C" and qc == 0:
            r.w("sync", "s_hoevac", 16)
            r.do("sync", lambda e: e.dma_start(dbg["ho"][:, :], RD[:, 0:8192]),
                 "d_out", 16)
            r.w("sync", "s_r1v", 32)
            for qt in range(4):
                r.do("sync",
                     (lambda qt=qt: (lambda e: e.dma_start(
                         dbg["r1"][qt * 128:(qt + 1) * 128, :],
                         r1_ap(qt, 0, 2048))))(), "d_out", 16)
        for qt in range(4):
            gqt = qc * 4 + qt
            r.w("scalar", "s_r1v", qc * 32 + 29 + qt)
            if gqt >= 4:
                r.w("scalar", "s_y1f", gqt - 3)
            r.do("scalar",
                 (lambda qt=qt: (lambda e: e.activation(
                     PTB[:, :], r1_ap(qt, 0, 2048), AF.Copy, scale=RMEAN,
                     accum_out=mI4(qt))))(),
                 "s_stat", 1)
            r.do("scalar",
                 (lambda qt=qt: (lambda e: e.activation(
                     PTB[:, :], r1_ap(qt, 0, 2048), AF.Square, scale=RSQ,
                     accum_out=e24(qt))))(),
                 "s_stat", 1)
            r.w("vector", "s_stat", 2 * (gqt + 1))
            r.do("vector",
                 (lambda qt=qt: (lambda e: e.scalar_tensor_tensor(
                     nv4(qt), mI4(qt), mI4(qt), e24(qt), ALU.mult, ALU.subtract)))(),
                 "s_nvpe", 1)
            r.w("scalar", "s_nvpe", gqt + 1)
            r.do("scalar",
                 (lambda qt=qt: (lambda e: e.activation(
                     sd4(qt), nv4(qt), AF.Sqrt, scale=-1.0,
                     bias=LNS[:, 20:21])))(),
                 "s_std", 1)
            r.w("vector", "s_std", gqt + 1)
            r.do("vector",
                 (lambda qt=qt: (lambda e: e.reciprocal(rs4(qt), sd4(qt))))())
            for _ in range(3):
                r.do("vector", lambda e: e.tensor_copy(JK[:, :], JK[:, :]))
            if gqt >= 2:
                r.w("vector", "d_y1f%d" % (gqt % 2), 16 * (gqt // 2))
                r.w("vector", "d_y1b%d" % (gqt % 2), 16 * (gqt // 2))
            r.do("vector",
                 (lambda qt=qt, gqt=gqt: (lambda e: e.tensor_scalar(
                     y1f_ap(gqt % 2), r1_ap(qt, 0, 2048), mI4(qt), rs4(qt),
                     ALU.subtract, ALU.mult)))(),
                 "s_y1f", 1)
            r.do("vector",
                 (lambda gqt=gqt: (lambda e: e.tensor_copy(
                     Y1B[:, (gqt % 2) * 2048:(gqt % 2 + 1) * 2048], y1f_ap(gqt % 2))))(),
                 "s_y1b", 1)
            r.w("sync", "s_y1f", gqt + 1)
            r.do("sync",
                 (lambda gqt=gqt: (lambda e: e.dma_start(
                     y1f_hbm[gqt * 128:(gqt + 1) * 128, :], y1f_ap(gqt % 2))))(),
                 "d_y1f%d" % (gqt % 2), 16)
            r.w("sync", "s_y1b", gqt + 1)
            r.do("sync",
                 (lambda gqt=gqt: (lambda e: e.dma_start(
                     y1b_hbm[gqt * 128:(gqt + 1) * 128, :],
                     Y1B[:, (gqt % 2) * 2048:(gqt % 2 + 1) * 2048])))(),
                 "d_y1b%d" % (gqt % 2), 16)

    if upto == "C":
        r.w("sync", "d_y1f0", 16 * 8)
        r.w("sync", "d_y1f1", 16 * 8)
        r.w("sync", "d_y1b0", 16 * 8)
        r.w("sync", "d_y1b1", 16 * 8)
        return

    r.w("sync", "s_stat", 32)
    r.w("sync", "s_y1b", 16)
    r.w("sync", "s_hoevac", 64)
    r.w("sync", "s_repc", 64)
    r.w("sync", "d_y1b0", 16 * 8)
    r.w("sync", "d_y1b1", 16 * 8)
    r.w("sync", "d_y1f0", 16 * 8)
    r.w("sync", "d_y1f1", 16 * 8)
    for i in range(DT):
        r.do("sync",
             (lambda i=i: (lambda e: e.dma_start(
                 RA[:, i * 2048:(i + 1) * 2048],
                 y1b_hbm[:, i * 128:(i + 1) * 128], transpose=True)))(),
             "d_y1T", 16)

    for g in range(NG):
        if g == 0:
            r.w("gpsimd", "s_att", 128)
        if g >= 2:
            r.w("gpsimd", "sD1_g", g - 1)
        r.do("gpsimd",
             (lambda g=g: (lambda e: e.dma_start(
                 wg_ap(g % 2, 0, 4096), wg_h[:, g * 4096:(g + 1) * 4096])))(),
             "d_wg%d" % (g % 2), 16)
        r.do("gpsimd",
             (lambda g=g: (lambda e: e.dma_start(
                 wu_ap(g % 2, 0, 4096), wu_h[:, g * 4096:(g + 1) * 4096])))(),
             "d_wg%d" % (g % 2), 16)
        if g == 3:
            r.do("gpsimd", lambda e: e.dma_start(
                wd_sb(0, 0, 16384), wd_h[:, 0:16384]), "d_wd0", 16)
        if g == 16:
            r.do("gpsimd", lambda e: e.dma_start(
                wd_sb(1, 0, 16384), wd_h[:, 16384:32768]), "d_wd1", 16)

    nsil = 0
    ngat = 0
    for g in range(NG):
        if g == 0:
            r.w("tensor", "d_y1T", 16 * 16)
            r.w("tensor", "s_r1v", 128)
        r.w("tensor", "d_wg%d" % (g % 2), 32 * (g // 2 + 1))
        for j in range(2):
            ht = 2 * g + j
            for tc in range(4):
                sidx = ht * 4 + tc
                gb_off = (tc % 2) * 512
                r.w("tensor", "s_silu", sidx - 1)
                for i in range(DT):
                    r.do("tensor",
                         (lambda i=i, j=j, g=g, tc=tc, gb_off=gb_off: (lambda e: e.matmul(
                             stA[:, gb_off:gb_off + 512],
                             wg_ap(g % 2, i * 256 + j * 128, i * 256 + (j + 1) * 128),
                             RA[:, i * 2048 + tc * 512: i * 2048 + (tc + 1) * 512],
                             start=(i == 0), stop=(i == DT - 1))))(),
                         "s_gate" if i == DT - 1 else None)
                r.w("tensor", "s_gated", sidx - 1)
                for i in range(DT):
                    r.do("tensor",
                         (lambda i=i, j=j, g=g, tc=tc, gb_off=gb_off: (lambda e: e.matmul(
                             stB[:, gb_off:gb_off + 512],
                             wu_ap(g % 2, i * 256 + j * 128, i * 256 + (j + 1) * 128),
                             RA[:, i * 2048 + tc * 512: i * 2048 + (tc + 1) * 512],
                             start=(i == 0), stop=(i == DT - 1))))(),
                         "s_up" if i == DT - 1 else None)
                nsil += 1
                r.w("scalar", "s_gate", nsil)
                r.w("scalar", "s_gated", nsil - 2)
                r.do("scalar",
                     (lambda tc=tc, gb_off=gb_off: (lambda e: e.activation(
                         silu_ap(tc % 2), stA[:, gb_off:gb_off + 512], AF.Silu)))(),
                     "s_silu", 1)
                ngat += 1
                r.w("vector", "s_silu", ngat)
                r.w("vector", "s_up", ngat)
                if g >= 1 and j == 0 and tc == 0:
                    r.w("vector", "d_gw%d" % ((g - 1) % 2), 16 * 8 * ((g - 1) // 2 + 1))
                r.do("vector",
                     (lambda j=j, tc=tc, gb_off=gb_off: (lambda e: e.tensor_tensor(
                         gstage_ap(j, tc * 512, (tc + 1) * 512),
                         silu_ap(tc % 2), stB[:, gb_off:gb_off + 512], ALU.mult)))(),
                     "s_gated", 1)
        r.do("tensor", (lambda: (lambda e: e.matmul(
            sr0[0:1, 0:1], ONE[:, :], ONE[:, 0:1], start=True, stop=True)))(),
            "sD1_g", 1)
        r.w("sync", "s_gated", 8 * (g + 1))
        for tc2 in range(8):
            r.do("sync",
                 (lambda g=g, tc2=tc2: (lambda e: e.dma_start(
                     AP(gat_hbm, (tc2 * HTN + 2 * g) * 32768,
                        [[256, 128], [32768, 2], [1, 256]]),
                     AP(RE, tc2 * 256, [[8192, 128], [2048, 2], [1, 256]]))))(),
                 "d_gw%d" % (g % 2), 16)

    if upto == "D1":
        r.w("sync", "d_gw0", 16 * 8 * 16)
        r.w("sync", "d_gw1", 16 * 8 * 16)
        return

    r.w("tensor", "s_hoevac", 64)
    r.w("tensor", "s_repc", 64)
    r.w("tensor", "s_recip", 64)
    for dc in range(NDC):
        if dc >= 2:
            r.w("gpsimd", "s_down", (dc - 1) * 16)
            r.do("gpsimd",
                 (lambda dc=dc: (lambda e: e.dma_start(
                     wd_sb(dc % 2, 0, 16384),
                     wd_h[:, dc * 16384:(dc + 1) * 16384])))(),
                 "d_wd%d" % (dc % 2), 16)
        for tc2 in range(8):
            idx_g = dc * 8 + tc2
            if idx_g < 2:
                r.w("sync", "d_gw0", 16 * 8 * 16)
                r.w("sync", "d_gw1", 16 * 8 * 16)
                r.w("sync", "sD1_g", 32)
            else:
                r.w("sync", "s_down", (idx_g - 1) * 2)
            r.do("sync",
                 (lambda dc=dc, tc2=tc2, idx_g=idx_g: (lambda e: e.dma_start(
                     AP(RA, (idx_g % 2) * 16384,
                        [[32768, 128], [256, 64], [1, 256]]),
                     AP(gat_hbm, tc2 * HTN * 32768,
                        [[256, 128], [32768, 64], [1, 256]]))))(),
                 "d_gr%d" % (idx_g % 2), 16)
            for tsub in range(2):
                tt = tc2 * 2 + tsub
                idx_d = dc * 16 + tt
                r.w("sync", "s_r2v", idx_d - 1)
                r.do("sync",
                     (lambda dc=dc, tt=tt, idx_d=idx_d: (lambda e: e.dma_start(
                         y1c_ap(idx_d % 2),
                         y1f_hbm[tt * 128:(tt + 1) * 128, dc * 256:(dc + 1) * 256])))(),
                     "d_y1c%d" % (idx_d % 2), 16)
                r.w("tensor", "d_wd%d" % (dc % 2), 16 * (dc // 2 + 1))
                r.w("tensor", "d_gr%d" % (idx_g % 2), 16 * (idx_g // 2 + 1))
                r.w("tensor", "s_r2v", idx_d - 1)
                db = (ho0, ho1)[idx_d % 2]
                for ht in range(HTN):
                    r.do("tensor",
                         (lambda ht=ht, dc=dc, tsub=tsub, idx_g=idx_g, db=db:
                          (lambda e: e.matmul(
                              db[:, 0:256],
                              gat_sb(idx_g % 2, ht * 256 + tsub * 128,
                                     ht * 256 + (tsub + 1) * 128),
                              wd_sb(dc % 2, ht * 256, (ht + 1) * 256),
                              start=(ht == 0), stop=(ht == HTN - 1))))(),
                         "s_down" if ht == HTN - 1 else None)
                r.w("vector", "s_down", idx_d + 1)
                r.w("vector", "d_y1c%d" % (idx_d % 2), 16 * (idx_d // 2 + 1))
                if dc < 7:
                    if idx_d >= 2:
                        r.w("vector", "d_r2w%d" % (idx_d % 2), 16 * ((idx_d - 2) // 2 + 1))
                    r.do("vector",
                         (lambda idx_d=idx_d, db=db: (lambda e: e.tensor_tensor(
                             r2c_ap(idx_d % 2), db[:, 0:256], y1c_ap(idx_d % 2),
                             ALU.add)))(),
                         "s_r2v", 1)
                    r.w("sync", "s_r2v", idx_d + 1)
                    r.do("sync",
                         (lambda dc=dc, tt=tt, idx_d=idx_d: (lambda e: e.dma_start(
                             r2_hbm[tt * 128:(tt + 1) * 128, dc * 256:(dc + 1) * 256],
                             r2c_ap(idx_d % 2))))(),
                         "d_r2w%d" % (idx_d % 2), 16)
                else:
                    r.w("sync", "d_r2w0", 16 * 56)
                    r.w("sync", "d_r2w1", 16 * 56)
                    if tt < 2:
                        r.w("sync", "sD1_g", 32)
                    r.w("sync", "s_y2", tt - 1)
                    r.do("sync",
                         (lambda tt=tt: (lambda e: e.dma_start(
                             r2f_ap(tt % 2, 0, 1792),
                             r2_hbm[tt * 128:(tt + 1) * 128, :])))(),
                         "d_r2r%d" % (tt % 2), 16)
                    r.do("vector",
                         (lambda tt=tt, db=db, idx_d=idx_d: (lambda e: e.tensor_tensor(
                             r2f_ap(tt % 2, 1792, 2048), db[:, 0:256],
                             y1c_ap(idx_d % 2), ALU.add)))(),
                         "s_r2v", 1)
                    r.w("scalar", "s_r2v", 112 + tt + 1)
                    r.w("scalar", "d_r2r%d" % (tt % 2), 16 * (tt // 2 + 1))
                    if tt >= 4:
                        r.w("scalar", "s_y2", tt - 3)
                    r.do("scalar",
                         (lambda tt=tt: (lambda e: e.activation(
                             PTB[:, :], r2f_ap(tt % 2, 0, 2048), AF.Copy,
                             scale=RMEAN, accum_out=mI4(tt % 4))))(),
                         "s_stat2", 1)
                    r.do("scalar",
                         (lambda tt=tt: (lambda e: e.activation(
                             PTB[:, :], r2f_ap(tt % 2, 0, 2048), AF.Square,
                             scale=RSQ, accum_out=e24(tt % 4))))(),
                         "s_stat2", 1)
                    r.w("vector", "s_stat2", 2 * (tt + 1))
                    r.do("vector",
                         (lambda tt=tt: (lambda e: e.scalar_tensor_tensor(
                             nv4(tt % 4), mI4(tt % 4), mI4(tt % 4), e24(tt % 4),
                             ALU.mult, ALU.subtract)))(),
                         "s_nvpe2", 1)
                    r.w("scalar", "s_nvpe2", tt + 1)
                    r.do("scalar",
                         (lambda tt=tt: (lambda e: e.activation(
                             sd4(tt % 4), nv4(tt % 4), AF.Sqrt, scale=-1.0,
                             bias=LNS[:, 20:21])))(),
                         "s_std2", 1)
                    r.w("vector", "s_std2", tt + 1)
                    r.do("vector",
                         (lambda tt=tt: (lambda e: e.reciprocal(rs4(tt % 4), sd4(tt % 4))))())
                    for _ in range(3):
                        r.do("vector", lambda e: e.tensor_copy(JK[:, :], JK[:, :]))
                    r.w("vector", "d_out", 16 * (tt - 1))
                    r.do("vector",
                         (lambda tt=tt: (lambda e: e.tensor_scalar(
                             y2_ap(tt % 2), r2f_ap(tt % 2, 0, 2048), mI4(tt % 4),
                             rs4(tt % 4), ALU.subtract, ALU.mult)))(),
                         "s_y2", 1)
                    r.w("sync", "s_y2", tt + 1)
                    r.do("sync",
                         (lambda tt=tt: (lambda e: e.dma_start(
                             y_h[tt * 128:(tt + 1) * 128, :], y2_ap(tt % 2))))(),
                         "d_out", 16)

    r.w("sync", "d_out", 16 * 16)


def tile_pm(w):
    R, C = w.shape
    n = R // 128
    return np.ascontiguousarray(w.reshape(n, 128, C).transpose(1, 0, 2).reshape(128, n * C))


def regroup(w128, n, k, m):
    return np.ascontiguousarray(
        w128.reshape(128, n, k, m).transpose(0, 2, 1, 3).reshape(128, n * k * m))


def make_core_inputs(x, wq, wk, wv, wo, wg, wu, wd):
    out = {}
    out["wq_h"] = tile_pm(wq)
    out["wk_h"] = tile_pm(wk)
    out["wv_h"] = tile_pm(wv)
    out["wo_h"] = regroup(tile_pm(wo), 16, 8, 256)
    wgT = np.ascontiguousarray(wg.T)
    wuT = np.ascontiguousarray(wu.T)
    wdT = np.ascontiguousarray(wd.T)
    out["wg_h"] = regroup(tile_pm(wgT), 16, 32, 256)
    out["wu_h"] = regroup(tile_pm(wuT), 16, 32, 256)
    out["wd_h"] = regroup(tile_pm(wdT), 64, 8, 256)
    out["id_h"] = np.eye(128, dtype=ml_dtypes.bfloat16)
    return out


def make_xt(xb, half):
    other = 1 - half
    xperm = np.concatenate([xb[other * 2048:(other + 1) * 2048],
                            xb[half * 2048:(half + 1) * 2048]], axis=0)
    xT = np.ascontiguousarray(xperm.T)
    return regroup(tile_pm(xT), 16, 16, 256)


def kernel(L0_attn_Q_input, L0_attn_Wq, L0_attn_Wk, L0_attn_Wv, L0_attn_Wo,
           L0_ffn_w_gate, L0_ffn_w_up, L0_ffn_w_down):
    from concourse.bass_utils import run_bass_kernel_spmd

    x = np.ascontiguousarray(np.asarray(L0_attn_Q_input, dtype=np.float32))
    wq = np.ascontiguousarray(np.asarray(L0_attn_Wq, dtype=np.float32))
    wk = np.ascontiguousarray(np.asarray(L0_attn_Wk, dtype=np.float32))
    wv = np.ascontiguousarray(np.asarray(L0_attn_Wv, dtype=np.float32))
    wo = np.ascontiguousarray(np.asarray(L0_attn_Wo, dtype=np.float32))
    wg = np.ascontiguousarray(np.asarray(L0_ffn_w_gate, dtype=np.float32))
    wu = np.ascontiguousarray(np.asarray(L0_ffn_w_up, dtype=np.float32))
    wd = np.ascontiguousarray(np.asarray(L0_ffn_w_down, dtype=np.float32))

    shared = make_core_inputs(x, wq, wk, wv, wo, wg, wu, wd)
    in_maps = []
    for core in range(8):
        b, half = core // 2, core % 2
        m = dict(shared)
        m["xt_h"] = make_xt(x[b], half)
        m["xn_h"] = np.ascontiguousarray(x[b, half * 2048:(half + 1) * 2048])
        in_maps.append(m)

    nc = build(upto="full", debug=False)
    res = run_bass_kernel_spmd(nc, in_maps, core_ids=list(range(8)))

    out = np.empty((4, 4096, 2048), dtype=np.float32)
    for core in range(8):
        b, half = core // 2, core % 2
        out[b, half * 2048:(half + 1) * 2048] = np.asarray(
            res.results[core]["y_h"], dtype=np.float32)
    return out


# version 02
# speedup vs baseline: 1.2502x; 1.2502x over previous
import numpy as np
import ml_dtypes
import concourse.bass as bass
import concourse.mybir as mybir
from contextlib import ExitStack

F32 = mybir.dt.float32
BF16 = mybir.dt.bfloat16
AF = mybir.ActivationFunctionType
ALU = mybir.AluOpType
AX = mybir.AxisListType
AP = bass.AP

D = 2048
DT = 16
H = 16
SF = 4096
NHC = 16
T = 2048
NQC = 4
HID = 8192
HTN = 64
NG = 32
NDC = 8
SCALE = float(1.0 / np.sqrt(128.0))
EPS = 1e-5
RMEAN = float(1.0 / 2048.0)
RSQ = float(1.0 / np.sqrt(2048.0))

ENGS = ("tensor", "scalar", "vector", "sync", "gpsimd")


class Rec:
    def __init__(self):
        self.ops = {k: [] for k in ENGS}
        self.cnt = {}

    def w(self, eng, sem, val):
        if val > 0:
            self.ops[eng].append(("w", sem, val))

    def do(self, eng, fn, sem=None, amt=1):
        if sem is not None:
            self.cnt[sem] = self.cnt.get(sem, 0) + amt
        self.ops[eng].append(("c", fn, sem, amt))

    def c(self, sem):
        return self.cnt.get(sem, 0)

    def scope(self, name):
        for eng in ENGS:
            self.ops[eng].append(("s", name))

    def replay(self, eng, e, sems, nc=None):
        cur = None
        for item in self.ops[eng]:
            if item[0] == "w":
                e.wait_ge(sems[item[1]], item[2])
            elif item[0] == "s":
                if nc is not None:
                    if cur is not None:
                        nc.leave_named_scope(cur[0], cur[1], False)
                    sid, _ = nc.enter_named_scope(item[1], False)
                    cur = (item[1], sid)
            else:
                _, fn, sem, amt = item
                ins = fn(e)
                if sem is not None:
                    ins.then_inc(sems[sem], amt)
        if cur is not None and nc is not None:
            nc.leave_named_scope(cur[0], cur[1], False)


SEM_NAMES = [
    "d_wkv", "d_wq", "d_xt0", "d_xt1", "d_wo0", "d_wo1", "d_wg0", "d_wg1",
    "d_wd0", "d_wd1", "d_id", "d_xn0", "d_xn1", "d_xn2", "d_xn3",
    "d_y1f0", "d_y1f1",
    "d_y1b0", "d_y1b1", "d_y1T", "d_gw0", "d_gw1", "d_gr0", "d_gr1",
    "d_y1c0", "d_y1c1", "d_y1c2", "d_y1c3", "d_r2w0", "d_r2w1", "d_r2w2",
    "d_r2w3", "d_r2r0", "d_r2r1", "d_out",
    "s_init", "s_Ak", "s_Akf", "s_Av", "s_Avt", "s_Atr", "s_Avn",
    "s_Aq", "s_Aqf",
    "s_st", "s_exp", "s_ho", "s_sums", "s_recip", "s_rep", "s_repc", "s_hoevac",
    "s_att", "s_r1v", "s_stat", "s_nvpe", "s_std", "s_y1f", "s_y1b",
    "s_gate", "s_up", "s_silu", "s_gated", "sD1_g",
    "s_down", "s_r2v", "s_stat2", "s_nvpe2", "s_std2", "s_y2", "s_tr", "s_trf",
]


def build(upto="full", debug=False):
    nc = bass.Bass("TRN2", target_bir_lowering=False)

    xt_h = nc.dram_tensor("xt_h", [128, NHC * DT * 256], F32, kind="ExternalInput")
    xn_h = nc.dram_tensor("xn_h", [T, D], F32, kind="ExternalInput")
    wq_h = nc.dram_tensor("wq_h", [128, DT * D], F32, kind="ExternalInput")
    wk_h = nc.dram_tensor("wk_h", [128, DT * 128], F32, kind="ExternalInput")
    wv_h = nc.dram_tensor("wv_h", [128, DT * 128], F32, kind="ExternalInput")
    wo_h = nc.dram_tensor("wo_h", [128, 8 * 16 * 256], F32, kind="ExternalInput")
    wg_h = nc.dram_tensor("wg_h", [128, NG * DT * 256], F32, kind="ExternalInput")
    wu_h = nc.dram_tensor("wu_h", [128, NG * DT * 256], F32, kind="ExternalInput")
    wd_h = nc.dram_tensor("wd_h", [128, NDC * HTN * 256], F32, kind="ExternalInput")
    id_h = nc.dram_tensor("id_h", [128, 128], BF16, kind="ExternalInput")

    skind = "ExternalOutput" if debug else "Internal"
    y1f_hbm = nc.dram_tensor("y1f_hbm", [T, D], F32, kind=skind)
    y1b_hbm = nc.dram_tensor("y1b_hbm", [T, D], BF16, kind=skind)
    gat_hbm = nc.dram_tensor("gat_hbm", [8 * HTN, 128, 256], BF16, kind=skind)
    r2_hbm = nc.dram_tensor("r2_hbm", [T, 7 * 256], F32, kind="Internal")
    y_h = nc.dram_tensor("y_h", [T, D], F32, kind="ExternalOutput")

    dbg = {}
    if debug:
        for nm, shp, dt in [("kT", [128, SF], BF16), ("vN", [128, SF], BF16),
                            ("qT", [128, H * T], BF16), ("ho", [128, 8192], BF16),
                            ("r1", [512, D], F32)]:
            dbg[nm] = nc.dram_tensor("dbg_" + nm, shp, dt, kind="ExternalOutput")

    with ExitStack() as ctx:
        E = ctx.enter_context
        RA = E(nc.sbuf_tensor("RA", [128, 32768], BF16))
        RB = E(nc.sbuf_tensor("RB", [128, 32768], BF16))
        RC = E(nc.sbuf_tensor("RC", [128, 8192], BF16))
        RD = E(nc.sbuf_tensor("RD", [128, 16384], BF16))
        RE = E(nc.sbuf_tensor("RE", [128, 8192], BF16))
        PTB = E(nc.sbuf_tensor("PTB", [128, 3072], BF16))
        Y1B = E(nc.sbuf_tensor("Y1B", [128, 4096], BF16))
        IDN = E(nc.sbuf_tensor("IDN", [128, 128], BF16))
        ONE = E(nc.sbuf_tensor("ONE", [128, 1], BF16))
        ONF = E(nc.sbuf_tensor("ONF", [1, 128], F32))
        LNS = E(nc.sbuf_tensor("LNS", [128, 24], F32))

        stA = E(nc.psum_tensor("stA", [128, 1024], F32))
        stB = E(nc.psum_tensor("stB", [128, 1024], F32))
        ho0 = E(nc.psum_tensor("ho0", [128, 512], F32))
        ho1 = E(nc.psum_tensor("ho1", [128, 512], F32))
        sr0 = E(nc.psum_tensor("sr0", [128, 512], F32))
        sr1 = E(nc.psum_tensor("sr1", [128, 512], F32))

        sems = {n: E(nc.semaphore(n)) for n in SEM_NAMES}

        r = Rec()
        _emit(r, upto, debug, dict(
            xt_h=xt_h, xn_h=xn_h, wq_h=wq_h, wk_h=wk_h, wv_h=wv_h, wo_h=wo_h,
            wg_h=wg_h, wu_h=wu_h, wd_h=wd_h, id_h=id_h,
            y1f_hbm=y1f_hbm, y1b_hbm=y1b_hbm, gat_hbm=gat_hbm, r2_hbm=r2_hbm,
            y_h=y_h, dbg=dbg,
            RA=RA, RB=RB, RC=RC, RD=RD, RE=RE, PTB=PTB, Y1B=Y1B, IDN=IDN,
            ONE=ONE, ONF=ONF, LNS=LNS,
            stA=stA, stB=stB, ho0=ho0, ho1=ho1, sr0=sr0, sr1=sr1))

        block = E(nc.Block())

        @block.tensor
        def _(e):
            r.replay("tensor", e, sems, nc)

        @block.scalar
        def _(e):
            r.replay("scalar", e, sems, nc)

        @block.vector
        def _(e):
            r.replay("vector", e, sems, nc)

        @block.sync
        def _(e):
            r.replay("sync", e, sems, nc)

        @block.gpsimd
        def _(e):
            r.replay("gpsimd", e, sems, nc)

    return nc


def _emit(r, upto, debug, t):
    xt_h, xn_h = t["xt_h"], t["xn_h"]
    wq_h, wk_h, wv_h, wo_h = t["wq_h"], t["wk_h"], t["wv_h"], t["wo_h"]
    wg_h, wu_h, wd_h, id_h = t["wg_h"], t["wu_h"], t["wd_h"], t["id_h"]
    y1f_hbm, y1b_hbm = t["y1f_hbm"], t["y1b_hbm"]
    gat_hbm, r2_hbm, y_h = t["gat_hbm"], t["r2_hbm"], t["y_h"]
    dbg = t["dbg"]
    RA, RB, RC, RD, RE = t["RA"], t["RB"], t["RC"], t["RD"], t["RE"]
    PTB, Y1B, IDN, ONE, ONF = t["PTB"], t["Y1B"], t["IDN"], t["ONE"], t["ONF"]
    LNS = t["LNS"]
    stA, stB = t["stA"], t["stB"]
    ho0, ho1, sr0, sr1 = t["ho0"], t["ho1"], t["sr0"], t["sr1"]

    RAf = RA[:, :].bitcast(F32)
    RCf = RC[:, :].bitcast(F32)
    RDf = RD[:, :].bitcast(F32)
    REf = RE[:, :].bitcast(F32)

    def kT(a, b):
        return RE[:, a:b]

    def vN(a, b):
        return RE[:, 4096 + a:4096 + b]

    def r1_ap(qt, a, b):
        return RAf[:, qt * 2048 + a: qt * 2048 + b]

    def y1f_ap(par):
        return RAf[:, 8192 + par * 2048: 8192 + (par + 1) * 2048]

    def rep_ap(par):
        return RAf[:, 13312 + par * 512: 13312 + (par + 1) * 512]

    def xnt_ap(par):
        return RAf[:, 12288 + par * 256: 12288 + (par + 1) * 256]

    def rcp_ap(par):
        return RAf[0:1, 14336 + par * 512: 14336 + (par + 1) * 512]

    JK1 = RAf[:, 15360:15616]
    JK2 = RDf[:, 4096:4352]

    def xt_ap(par, a, b):
        return RC[:, par * 4096 + a: par * 4096 + b]

    def wo_ap(par, a, b):
        return RC[:, par * 4096 + a: par * 4096 + b]

    wk_sb = RD[:, 0:2048]
    wv_sb = RD[:, 2048:4096]

    def vt_ap(par):
        return RD[:, 4096 + par * 256: 4096 + (par + 1) * 256]

    def hoT_ap(par, a, b):
        return RD[:, par * 8192 + a: par * 8192 + b]

    def wg_ap(par, a, b):
        return RD[:, par * 4096 + a: par * 4096 + b]

    def wu_ap(par, a, b):
        return RC[:, par * 4096 + a: par * 4096 + b]

    def gstage_ap(j, a, b):
        return RE[:, j * 2048 + a: j * 2048 + b]

    def silu_ap(par):
        return REf[:, 2048 + par * 512: 2048 + (par + 1) * 512]

    def y1c_ap(par):
        return REf[:, par * 256: (par + 1) * 256]

    def r2c_ap(par):
        return REf[:, 1024 + par * 256: 1024 + (par + 1) * 256]

    def r2f_ap(par, a, b):
        return RCf[:, par * 2048 + a: par * 2048 + b]

    def y2_ap(par):
        return RDf[:, par * 2048: (par + 1) * 2048]

    def gat_sb(par, a, b):
        return RB[:, par * 16384 + a: par * 16384 + b]

    def wd_sb(par, a, b):
        return RA[:, par * 16384 + a: par * 16384 + b]

    mI4 = lambda i: LNS[:, i:i + 1]
    e24 = lambda i: LNS[:, 4 + i:5 + i]
    nv4 = lambda i: LNS[:, 8 + i:9 + i]
    sd4 = lambda i: LNS[:, 12 + i:13 + i]
    rs4 = lambda i: LNS[:, 16 + i:17 + i]

    r.scope("phA")
    r.do("gpsimd", lambda e: e.dma_start(wk_sb, wk_h[:, :]), "d_wkv", 16)
    r.do("gpsimd", lambda e: e.dma_start(wv_sb, wv_h[:, :]), "d_wkv", 16)
    r.do("gpsimd", lambda e: e.dma_start(RA[:, :], wq_h[:, :]), "d_wq", 16)
    r.do("sync", lambda e: e.dma_start(IDN[:, :], id_h[:, :]), "d_id", 16)
    r.do("vector", lambda e: e.memset(ONE[:, :], 1.0))
    r.do("vector", lambda e: e.memset(LNS[:, 20:21], EPS))
    r.do("vector", lambda e: e.memset(ONF[:, :], 1.0), "s_init", 1)

    for c in range(NHC):
        par = c % 2
        dsem = "d_xt%d" % par
        if c >= 2:
            cp = c - 2
            if cp < 8:
                r.w("gpsimd", "s_Av", cp + 1)
            else:
                r.w("gpsimd", "s_Aq", (cp - 8 + 1) * DT)
        r.do("gpsimd",
             (lambda c=c, par=par: (lambda e: e.dma_start(
                 xt_ap(par, 0, 4096), xt_h[:, c * 4096:(c + 1) * 4096])))(),
             dsem, 16)

        r.w("tensor", dsem, 16 * (c // 2 + 1))
        if c == 0:
            r.w("tensor", "d_wkv", 32)
            r.w("tensor", "d_id", 16)
        if c >= 1:
            r.w("tensor", "s_Akf", r.c("s_Akf"))
        for i in range(DT):
            r.do("tensor",
                 (lambda i=i, par=par: (lambda e: e.matmul(
                     stA[:, 512:768], wk_sb[:, i * 128:(i + 1) * 128],
                     xt_ap(par, i * 256, (i + 1) * 256),
                     start=(i == 0), stop=(i == DT - 1))))(),
                 "s_Ak" if i == DT - 1 else None)
        if c >= 1:
            r.w("tensor", "s_Avt", r.c("s_Avt"))
        for i in range(DT):
            r.do("tensor",
                 (lambda i=i, par=par: (lambda e: e.matmul(
                     stB[:, 512:768], wv_sb[:, i * 128:(i + 1) * 128],
                     xt_ap(par, i * 256, (i + 1) * 256),
                     start=(i == 0), stop=(i == DT - 1))))(),
                 "s_Av" if i == DT - 1 else None)
        r.w("vector", "s_Ak", r.c("s_Ak"))
        r.do("vector",
             (lambda c=c: (lambda e: e.tensor_copy(
                 kT(c * 256, (c + 1) * 256), stA[:, 512:768])))(),
             "s_Akf", 1)
        r.w("vector", "s_Av", r.c("s_Av"))
        if c >= 2:
            r.w("vector", "s_Atr", c - 1)
        r.do("vector",
             (lambda par=par: (lambda e: e.tensor_copy(vt_ap(par), stB[:, 512:768])))(),
             "s_Avt", 1)
        r.w("tensor", "s_Avt", r.c("s_Avt"))
        if c >= 2:
            r.w("tensor", "s_Avn", r.c("s_Avn") - 1)
        hob = (ho0, ho1)[par]
        for j in range(2):
            r.do("tensor",
                 (lambda par=par, j=j, hob=hob: (lambda e: e.transpose(
                     hob[:, :].bitcast(BF16)[:, j * 128:(j + 1) * 128],
                     vt_ap(par)[:, j * 128:(j + 1) * 128], IDN[:, :])))(),
                 "s_Atr" if j == 1 else None)
        r.w("vector", "s_Atr", r.c("s_Atr"))
        r.do("vector",
             (lambda c=c, hob=hob: (lambda e: e.tensor_copy(
                 vN(c * 256, (c + 1) * 256), hob[:, :].bitcast(BF16)[:, 0:256])))(),
             "s_Avn", 1)
        if c >= 8:
            if c == 8:
                r.w("tensor", "d_wq", 16)
            for f in range(DT):
                gq = (c - 8) * DT + f
                qb = (stA, stB)[gq % 2]
                r.w("tensor", "s_Aqf", gq - 1)
                for i in range(DT):
                    r.do("tensor",
                         (lambda i=i, f=f, par=par, qb=qb: (lambda e: e.matmul(
                             qb[:, 0:256],
                             RA[:, i * 2048 + f * 128: i * 2048 + (f + 1) * 128],
                             xt_ap(par, i * 256, (i + 1) * 256),
                             start=(i == 0), stop=(i == DT - 1))))(),
                         "s_Aq" if i == DT - 1 else None)
                r.w("vector", "s_Aq", r.c("s_Aq"))
                r.do("vector",
                     (lambda c=c, f=f, qb=qb: (lambda e: e.tensor_copy(
                         RB[:, f * 2048 + (c - 8) * 256: f * 2048 + (c - 7) * 256],
                         qb[:, 0:256])))(),
                     "s_Aqf", 1)

    if upto == "A":
        r.w("sync", "s_Akf", NHC)
        r.w("sync", "s_Avn", NHC)
        r.w("sync", "s_Aqf", 128)
        r.do("sync", lambda e: e.dma_start(dbg["kT"][:, :], kT(0, 4096)), "d_out", 16)
        r.do("sync", lambda e: e.dma_start(dbg["vN"][:, :], vN(0, 4096)), "d_out", 16)
        r.do("sync", lambda e: e.dma_start(dbg["qT"][:, :], RB[:, :]), "d_out", 16)
        r.w("sync", "d_out", 48)
        return

    r.scope("phBC")
    r.w("tensor", "s_Akf", NHC)
    r.w("tensor", "s_Avn", NHC)
    r.w("tensor", "s_Aqf", 128)
    r.w("tensor", "s_init", 1)

    def emit_rep(idx):
        srb = (sr0, sr1)[idx % 2]
        qc_, h_ = idx // 16, idx % 16
        r.w("tensor", "s_recip", idx + 1)
        r.do("tensor",
             (lambda idx=idx, srb=srb: (lambda e: e.matmul(
                 srb[:, :], ONF[:, :],
                 rcp_ap(idx % 2),
                 start=True, stop=True)))(),
             "s_rep", 1)
        r.w("scalar", "s_rep", r.c("s_rep"))
        r.w("scalar", "s_hoevac", idx - 1)
        r.do("scalar",
             (lambda idx=idx, srb=srb: (lambda e: e.copy(rep_ap(idx % 2), srb[:, :])))(),
             "s_repc", 1)
        r.w("vector", "s_repc", r.c("s_repc"))
        r.w("vector", "s_ho", idx + 1)
        if qc_ >= 2:
            r.w("vector", "s_att", 32 * (qc_ - 1))
        r.do("vector",
             (lambda idx=idx, qc_=qc_, h_=h_: (lambda e: e.tensor_tensor(
                 hoT_ap(qc_ % 2, h_ * 512, (h_ + 1) * 512),
                 (ho0, ho1)[idx % 2][:, :], rep_ap(idx % 2), ALU.mult)))(),
             "s_hoevac", 1)

    def emit_pv_pair(idx, p, need_wait=True):
        gp = idx * 16 + p
        sl = gp % 3
        if need_wait:
            r.w("tensor", "s_exp", gp + 1)
        hob = (ho0, ho1)[idx % 2]
        srb = (sr0, sr1)[idx % 2]
        if p == 0 and idx >= 2:
            r.w("tensor", "s_hoevac", idx - 1)
            r.w("tensor", "s_repc", idx - 1)
        for sub in range(2):
            tt = 2 * p + sub
            r.do("tensor",
                 (lambda tt=tt, sl=sl, sub=sub, hob=hob: (lambda e: e.matmul(
                     hob[:, :], vN(tt * 128, (tt + 1) * 128),
                     PTB[:, sl * 1024 + sub * 512: sl * 1024 + (sub + 1) * 512],
                     start=(tt == 0), stop=(tt == 31))))(),
                 "s_ho" if tt == 31 else None)
        for sub in range(2):
            tt = 2 * p + sub
            r.do("tensor",
                 (lambda tt=tt, sl=sl, sub=sub, srb=srb: (lambda e: e.matmul(
                     srb[0:1, :], ONE[:, :],
                     PTB[:, sl * 1024 + sub * 512: sl * 1024 + (sub + 1) * 512],
                     start=(tt == 0), stop=(tt == 31))))(),
                 "s_sums" if tt == 31 else None)

    for qc in range(NQC):
        if qc == 0:
            r.w("gpsimd", "s_Aq", 128)
        for dc in range(8):
            gw = qc * 8 + dc
            if gw >= 2:
                gp_ = gw - 2
                r.w("gpsimd", "s_att", (gp_ // 8) * 32 + ((gp_ % 8) + 1) * 4)
            r.do("gpsimd",
                 (lambda dc=dc, gw=gw: (lambda e: e.dma_start(
                     wo_ap(gw % 2, 0, 4096), wo_h[:, dc * 4096:(dc + 1) * 4096])))(),
                 "d_wo%d" % (gw % 2), 16)
            for qt in range(4):
                gxn = qc * 32 + dc * 4 + qt
                r.w("gpsimd", "s_r1v", gxn - 3)
                r.do("gpsimd",
                     (lambda qc=qc, dc=dc, qt=qt, gxn=gxn: (lambda e: e.dma_start(
                         xnt_ap(gxn % 4),
                         xn_h[qc * 512 + qt * 128: qc * 512 + (qt + 1) * 128,
                              dc * 256:(dc + 1) * 256])))(),
                     "d_xn%d" % (gxn % 4), 16)
        if qc >= 1:
            r.w("tensor", "s_r1v", 32 * qc)
        for h in range(H):
            idx = qc * 16 + h
            for p in range(16):
                gp = idx * 16 + p
                stb = (stA, stB)[p % 2]
                r.w("tensor", "s_exp", gp - 1)
                for sub in range(2):
                    tt = 2 * p + sub
                    r.do("tensor",
                         (lambda tt=tt, h=h, qc=qc, stb=stb, sub=sub: (lambda e: e.matmul(
                             stb[:, sub * 512:(sub + 1) * 512],
                             kT(tt * 128, (tt + 1) * 128),
                             RB[:, h * 2048 + qc * 512: h * 2048 + (qc + 1) * 512],
                             start=True, stop=True)))(),
                         "s_st" if sub == 1 else None)
                if p == 1 and idx >= 1 and (idx % 16) != 0:
                    emit_rep(idx - 1)
                if p >= 2:
                    emit_pv_pair(idx, p - 2, need_wait=False)
            emit_pv_pair(idx, 14)
            emit_pv_pair(idx, 15)
            for p in range(16):
                gp = idx * 16 + p
                stb = (stA, stB)[p % 2]
                r.w("scalar", "s_st", gp + 1)
                r.do("scalar",
                     (lambda gp=gp, stb=stb: (lambda e: e.activation(
                         PTB[:, (gp % 3) * 1024:(gp % 3 + 1) * 1024], stb[:, :],
                         AF.Exp, scale=SCALE)))(),
                     "s_exp", 1)
            r.w("vector", "s_sums", idx + 1)
            r.w("vector", "s_rep", idx - 1)
            r.do("vector",
                 (lambda idx=idx: (lambda e: e.reciprocal(
                     rcp_ap(idx % 2),
                     (sr0, sr1)[idx % 2][0:1, :])))(),
                 "s_recip", 1)

        emit_rep(qc * 16 + 15)
        r.w("tensor", "s_exp", (qc + 1) * 256)
        r.w("tensor", "s_hoevac", (qc + 1) * 16)
        attrot = [stA[:, 0:256], stA[:, 512:768], stB[:, 0:256], stB[:, 512:768]]
        for dc in range(8):
            gw = qc * 8 + dc
            r.w("tensor", "d_wo%d" % (gw % 2), 16 * (gw // 2 + 1))
            for qt in range(4):
                g_att = qc * 32 + dc * 4 + qt
                ab = attrot[g_att % 4]
                r.w("tensor", "s_r1v", g_att - 3)
                for f in range(16):
                    r.do("tensor",
                         (lambda f=f, qt=qt, qc=qc, gw=gw, ab=ab: (lambda e: e.matmul(
                             ab,
                             hoT_ap(qc % 2, f * 512 + qt * 128, f * 512 + (qt + 1) * 128),
                             wo_ap(gw % 2, f * 256, (f + 1) * 256),
                             start=(f == 0), stop=(f == 15))))(),
                         "s_att" if f == 15 else None)
                r.w("vector", "s_att", g_att + 1)
                r.w("vector", "d_xn%d" % (g_att % 4), 16 * (g_att // 4 + 1))
                r.do("vector",
                     (lambda qt=qt, dc=dc, g_att=g_att, ab=ab: (lambda e: e.tensor_tensor(
                         r1_ap(qt, dc * 256, (dc + 1) * 256), ab,
                         xnt_ap(g_att % 4), ALU.add)))(),
                     "s_r1v", 1)
        if debug and upto == "C" and qc == 0:
            r.w("sync", "s_hoevac", 16)
            r.do("sync", lambda e: e.dma_start(dbg["ho"][:, :], RD[:, 0:8192]),
                 "d_out", 16)
            r.w("sync", "s_r1v", 32)
            for qt in range(4):
                r.do("sync",
                     (lambda qt=qt: (lambda e: e.dma_start(
                         dbg["r1"][qt * 128:(qt + 1) * 128, :],
                         r1_ap(qt, 0, 2048))))(), "d_out", 16)
        for qt in range(4):
            gqt = qc * 4 + qt
            r.w("scalar", "s_r1v", qc * 32 + 29 + qt)
            if gqt >= 4:
                r.w("scalar", "s_y1f", gqt - 3)
            r.do("scalar",
                 (lambda qt=qt: (lambda e: e.activation(
                     PTB[:, 0:2048], r1_ap(qt, 0, 2048), AF.Copy, scale=RMEAN,
                     accum_out=mI4(qt))))(),
                 "s_stat", 1)
            r.do("scalar",
                 (lambda qt=qt: (lambda e: e.activation(
                     PTB[:, 0:2048], r1_ap(qt, 0, 2048), AF.Square, scale=RSQ,
                     accum_out=e24(qt))))(),
                 "s_stat", 1)
            r.w("vector", "s_stat", 2 * (gqt + 1))
            r.do("vector",
                 (lambda qt=qt: (lambda e: e.scalar_tensor_tensor(
                     nv4(qt), mI4(qt), mI4(qt), e24(qt), ALU.mult, ALU.subtract)))(),
                 "s_nvpe", 1)
            r.w("scalar", "s_nvpe", gqt + 1)
            r.do("scalar",
                 (lambda qt=qt: (lambda e: e.activation(
                     sd4(qt), nv4(qt), AF.Sqrt, scale=-1.0,
                     bias=LNS[:, 20:21])))(),
                 "s_std", 1)
            r.w("vector", "s_std", gqt + 1)
            r.do("vector",
                 (lambda qt=qt: (lambda e: e.reciprocal(rs4(qt), sd4(qt))))())
            for _ in range(3):
                r.do("vector", lambda e: e.tensor_copy(JK1, JK1))
            if gqt >= 2:
                r.w("vector", "d_y1f%d" % (gqt % 2), 16 * (gqt // 2))
            r.do("vector",
                 (lambda qt=qt, gqt=gqt: (lambda e: e.tensor_scalar(
                     y1f_ap(gqt % 2), r1_ap(qt, 0, 2048), mI4(qt), rs4(qt),
                     ALU.subtract, ALU.mult)))(),
                 "s_y1f", 1)
            if gqt >= 2:
                r.w("vector", "d_y1b%d" % (gqt % 2), 16 * (gqt // 2))
            r.do("vector",
                 (lambda gqt=gqt: (lambda e: e.tensor_copy(
                     Y1B[:, (gqt % 2) * 2048:(gqt % 2 + 1) * 2048], y1f_ap(gqt % 2))))(),
                 "s_y1b", 1)
            r.w("sync", "s_y1f", gqt + 1)
            r.do("sync",
                 (lambda gqt=gqt: (lambda e: e.dma_start(
                     y1f_hbm[gqt * 128:(gqt + 1) * 128, :], y1f_ap(gqt % 2))))(),
                 "d_y1f%d" % (gqt % 2), 16)
            r.w("sync", "s_y1b", gqt + 1)
            r.do("sync",
                 (lambda gqt=gqt: (lambda e: e.dma_start(
                     y1b_hbm[gqt * 128:(gqt + 1) * 128, :],
                     Y1B[:, (gqt % 2) * 2048:(gqt % 2 + 1) * 2048])))(),
                 "d_y1b%d" % (gqt % 2), 16)
        r.w("sync", "d_y1b0", 16 * 2 * (qc + 1))
        r.w("sync", "d_y1b1", 16 * 2 * (qc + 1))
        r.w("sync", "s_st", (qc + 1) * 256)
        for i in range(DT):
            r.do("sync",
                 (lambda i=i, qc=qc: (lambda e: e.dma_start(
                     RB[:, i * 2048 + qc * 512: i * 2048 + (qc + 1) * 512],
                     y1b_hbm[qc * 512:(qc + 1) * 512, i * 128:(i + 1) * 128],
                     transpose=True)))(),
                 "d_y1T", 16)

    if upto == "C":
        r.w("sync", "d_y1f0", 16 * 8)
        r.w("sync", "d_y1f1", 16 * 8)
        return

    r.scope("phD1")

    for g in range(NG):
        if g == 0:
            r.w("gpsimd", "s_att", 96)
        elif g == 1:
            r.w("gpsimd", "s_att", 128)
        else:
            r.w("gpsimd", "sD1_g", g - 1)
        r.do("gpsimd",
             (lambda g=g: (lambda e: e.dma_start(
                 wg_ap(g % 2, 0, 4096), wg_h[:, g * 4096:(g + 1) * 4096])))(),
             "d_wg%d" % (g % 2), 16)
        if g == 0:
            r.w("gpsimd", "s_att", 124)
        r.do("gpsimd",
             (lambda g=g: (lambda e: e.dma_start(
                 wu_ap(g % 2, 0, 4096), wu_h[:, g * 4096:(g + 1) * 4096])))(),
             "d_wg%d" % (g % 2), 16)
        if g == 3:
            r.w("gpsimd", "s_stat", 32)
            r.w("gpsimd", "s_y1f", 16)
            r.w("gpsimd", "s_r1v", 128)
            r.w("gpsimd", "s_hoevac", 64)
            r.w("gpsimd", "d_y1f0", 16 * 8)
            r.w("gpsimd", "d_y1f1", 16 * 8)
            r.do("gpsimd", lambda e: e.dma_start(
                wd_sb(0, 0, 16384), wd_h[:, 0:16384]), "d_wd0", 16)
        if g == 16:
            r.do("gpsimd", lambda e: e.dma_start(
                wd_sb(1, 0, 16384), wd_h[:, 16384:32768]), "d_wd1", 16)

    nsil = 0
    ngat = 0
    for g in range(NG):
        if g == 0:
            r.w("tensor", "s_r1v", 128)
        r.w("tensor", "d_wg%d" % (g % 2), 32 * (g // 2 + 1))
        for j in range(2):
            ht = 2 * g + j
            for tc in range(4):
                sidx = ht * 4 + tc
                gb_off = (tc % 2) * 512
                if g == 0:
                    r.w("tensor", "d_y1T", 16 * 16 * (tc + 1))
                r.w("tensor", "s_silu", sidx - 1)
                for i in range(DT):
                    r.do("tensor",
                         (lambda i=i, j=j, g=g, tc=tc, gb_off=gb_off: (lambda e: e.matmul(
                             stA[:, gb_off:gb_off + 512],
                             wg_ap(g % 2, i * 256 + j * 128, i * 256 + (j + 1) * 128),
                             RB[:, i * 2048 + tc * 512: i * 2048 + (tc + 1) * 512],
                             start=(i == 0), stop=(i == DT - 1))))(),
                         "s_gate" if i == DT - 1 else None)
                r.w("tensor", "s_gated", sidx - 1)
                for i in range(DT):
                    r.do("tensor",
                         (lambda i=i, j=j, g=g, tc=tc, gb_off=gb_off: (lambda e: e.matmul(
                             stB[:, gb_off:gb_off + 512],
                             wu_ap(g % 2, i * 256 + j * 128, i * 256 + (j + 1) * 128),
                             RB[:, i * 2048 + tc * 512: i * 2048 + (tc + 1) * 512],
                             start=(i == 0), stop=(i == DT - 1))))(),
                         "s_up" if i == DT - 1 else None)
                nsil += 1
                r.w("scalar", "s_gate", nsil)
                r.w("scalar", "s_gated", nsil - 2)
                r.do("scalar",
                     (lambda tc=tc, gb_off=gb_off: (lambda e: e.activation(
                         silu_ap(tc % 2), stA[:, gb_off:gb_off + 512], AF.Silu)))(),
                     "s_silu", 1)
                ngat += 1
                r.w("vector", "s_silu", ngat)
                r.w("vector", "s_up", ngat)
                if g >= 1 and j == 0 and tc == 0:
                    r.w("vector", "d_gw%d" % ((g - 1) % 2), 16 * 8 * ((g - 1) // 2 + 1))
                r.do("vector",
                     (lambda j=j, tc=tc, gb_off=gb_off: (lambda e: e.tensor_tensor(
                         gstage_ap(j, tc * 512, (tc + 1) * 512),
                         silu_ap(tc % 2), stB[:, gb_off:gb_off + 512], ALU.mult)))(),
                     "s_gated", 1)
        r.do("tensor", (lambda: (lambda e: e.matmul(
            sr0[0:1, 0:1], ONE[:, :], ONE[:, 0:1], start=True, stop=True)))(),
            "sD1_g", 1)
        r.w("sync", "s_gated", 8 * (g + 1))
        for tc2 in range(8):
            r.do("sync",
                 (lambda g=g, tc2=tc2: (lambda e: e.dma_start(
                     AP(gat_hbm, (tc2 * HTN + 2 * g) * 32768,
                        [[256, 128], [32768, 2], [1, 256]]),
                     AP(RE, tc2 * 256, [[8192, 128], [2048, 2], [1, 256]]))))(),
                 "d_gw%d" % (g % 2), 16)

    if upto == "D1":
        r.w("sync", "d_gw0", 16 * 8 * 16)
        r.w("sync", "d_gw1", 16 * 8 * 16)
        return

    r.scope("phD2")
    r.w("tensor", "s_hoevac", 64)
    r.w("tensor", "s_repc", 64)
    r.w("tensor", "s_recip", 64)
    r.w("tensor", "s_silu", 256)
    r.w("tensor", "s_gated", 256)
    rot = [ho0[:, 0:256], ho1[:, 0:256], sr0[:, 0:256], sr1[:, 0:256],
           stA[:, 0:256], stB[:, 0:256]]

    def y1c_dma(j):
        dcj, ttj = j // 16, j % 16
        if j == 0:
            r.w("scalar", "d_gw0", 16 * 8 * 16)
            r.w("scalar", "d_gw1", 16 * 8 * 16)
        r.w("scalar", "s_r2v", j - 3)
        r.do("scalar",
             (lambda dcj=dcj, ttj=ttj, j=j: (lambda e: e.dma_start(
                 y1c_ap(j % 4),
                 y1f_hbm[ttj * 128:(ttj + 1) * 128, dcj * 256:(dcj + 1) * 256])))(),
             "d_y1c%d" % (j % 4), 16)

    for j in range(3):
        y1c_dma(j)
    for dc in range(NDC):
        if dc >= 2:
            r.w("gpsimd", "s_down", (dc - 1) * 16)
            r.do("gpsimd",
                 (lambda dc=dc: (lambda e: e.dma_start(
                     wd_sb(dc % 2, 0, 16384),
                     wd_h[:, dc * 16384:(dc + 1) * 16384])))(),
                 "d_wd%d" % (dc % 2), 16)
        for tc2 in range(8):
            idx_g = dc * 8 + tc2
            if idx_g < 2:
                r.w("sync", "d_gw0", 16 * 8 * 16)
                r.w("sync", "d_gw1", 16 * 8 * 16)
                r.w("sync", "sD1_g", 32)
            else:
                r.w("sync", "s_down", (idx_g - 1) * 2)
            r.do("sync",
                 (lambda dc=dc, tc2=tc2, idx_g=idx_g: (lambda e: e.dma_start(
                     AP(RB, (idx_g % 2) * 16384,
                        [[32768, 128], [256, 64], [1, 256]]),
                     AP(gat_hbm, tc2 * HTN * 32768,
                        [[256, 128], [32768, 64], [1, 256]]))))(),
                 "d_gr%d" % (idx_g % 2), 16)
            for tsub in range(2):
                tt = tc2 * 2 + tsub
                idx_d = dc * 16 + tt
                if idx_d + 3 < 128:
                    y1c_dma(idx_d + 3)
                r.w("tensor", "d_wd%d" % (dc % 2), 16 * (dc // 2 + 1))
                r.w("tensor", "d_gr%d" % (idx_g % 2), 16 * (idx_g // 2 + 1))
                r.w("tensor", "s_r2v", idx_d - 5)
                db = rot[idx_d % 6]
                for ht in range(HTN):
                    r.do("tensor",
                         (lambda ht=ht, dc=dc, tsub=tsub, idx_g=idx_g, db=db:
                          (lambda e: e.matmul(
                              db,
                              gat_sb(idx_g % 2, ht * 256 + tsub * 128,
                                     ht * 256 + (tsub + 1) * 128),
                              wd_sb(dc % 2, ht * 256, (ht + 1) * 256),
                              start=(ht == 0), stop=(ht == HTN - 1))))(),
                         "s_down" if ht == HTN - 1 else None)
                r.w("vector", "s_down", idx_d + 1)
                r.w("vector", "d_y1c%d" % (idx_d % 4), 16 * (idx_d // 4 + 1))
                if dc < 7:
                    if idx_d >= 4:
                        r.w("vector", "d_r2w%d" % (idx_d % 4), 16 * (idx_d // 4))
                    r.do("vector",
                         (lambda idx_d=idx_d, db=db: (lambda e: e.tensor_tensor(
                             r2c_ap(idx_d % 4), db, y1c_ap(idx_d % 4),
                             ALU.add)))(),
                         "s_r2v", 1)
                    r.w("scalar", "s_r2v", idx_d + 1)
                    r.do("scalar",
                         (lambda dc=dc, tt=tt, idx_d=idx_d: (lambda e: e.dma_start(
                             r2_hbm[tt * 128:(tt + 1) * 128, dc * 256:(dc + 1) * 256],
                             r2c_ap(idx_d % 4))))(),
                         "d_r2w%d" % (idx_d % 4), 16)
                else:
                    for pw in range(4):
                        r.w("scalar", "d_r2w%d" % pw, 16 * 28)
                    if tt < 2:
                        r.w("scalar", "sD1_g", 32)
                    r.w("scalar", "s_y2", tt - 1)
                    r.do("scalar",
                         (lambda tt=tt: (lambda e: e.dma_start(
                             r2f_ap(tt % 2, 0, 1792),
                             r2_hbm[tt * 128:(tt + 1) * 128, :])))(),
                         "d_r2r%d" % (tt % 2), 16)
                    r.do("vector",
                         (lambda tt=tt, db=db, idx_d=idx_d: (lambda e: e.tensor_tensor(
                             r2f_ap(tt % 2, 1792, 2048), db,
                             y1c_ap(idx_d % 4), ALU.add)))(),
                         "s_r2v", 1)
                    r.w("scalar", "s_r2v", 112 + tt + 1)
                    r.w("scalar", "d_r2r%d" % (tt % 2), 16 * (tt // 2 + 1))
                    if tt >= 4:
                        r.w("scalar", "s_y2", tt - 3)
                    r.do("scalar",
                         (lambda tt=tt: (lambda e: e.activation(
                             PTB[:, 0:2048], r2f_ap(tt % 2, 0, 2048), AF.Copy,
                             scale=RMEAN, accum_out=mI4(tt % 4))))(),
                         "s_stat2", 1)
                    r.do("scalar",
                         (lambda tt=tt: (lambda e: e.activation(
                             PTB[:, 0:2048], r2f_ap(tt % 2, 0, 2048), AF.Square,
                             scale=RSQ, accum_out=e24(tt % 4))))(),
                         "s_stat2", 1)
                    r.w("vector", "s_stat2", 2 * (tt + 1))
                    r.do("vector",
                         (lambda tt=tt: (lambda e: e.scalar_tensor_tensor(
                             nv4(tt % 4), mI4(tt % 4), mI4(tt % 4), e24(tt % 4),
                             ALU.mult, ALU.subtract)))(),
                         "s_nvpe2", 1)
                    r.w("scalar", "s_nvpe2", tt + 1)
                    r.do("scalar",
                         (lambda tt=tt: (lambda e: e.activation(
                             sd4(tt % 4), nv4(tt % 4), AF.Sqrt, scale=-1.0,
                             bias=LNS[:, 20:21])))(),
                         "s_std2", 1)
                    r.w("vector", "s_std2", tt + 1)
                    r.do("vector",
                         (lambda tt=tt: (lambda e: e.reciprocal(rs4(tt % 4), sd4(tt % 4))))())
                    for _ in range(3):
                        r.do("vector", lambda e: e.tensor_copy(JK2, JK2))
                    r.w("vector", "d_out", 16 * (tt - 1))
                    r.do("vector",
                         (lambda tt=tt: (lambda e: e.tensor_scalar(
                             y2_ap(tt % 2), r2f_ap(tt % 2, 0, 2048), mI4(tt % 4),
                             rs4(tt % 4), ALU.subtract, ALU.mult)))(),
                         "s_y2", 1)
                    r.w("scalar", "s_y2", tt + 1)
                    r.do("scalar",
                         (lambda tt=tt: (lambda e: e.dma_start(
                             y_h[tt * 128:(tt + 1) * 128, :], y2_ap(tt % 2))))(),
                         "d_out", 16)

    r.w("sync", "d_out", 16 * 16)


def tile_pm(w):
    R, C = w.shape
    n = R // 128
    return np.ascontiguousarray(w.reshape(n, 128, C).transpose(1, 0, 2).reshape(128, n * C))


def regroup(w128, n, k, m):
    return np.ascontiguousarray(
        w128.reshape(128, n, k, m).transpose(0, 2, 1, 3).reshape(128, n * k * m))


def make_core_inputs(x, wq, wk, wv, wo, wg, wu, wd):
    out = {}
    out["wq_h"] = tile_pm(wq)
    out["wk_h"] = tile_pm(wk)
    out["wv_h"] = tile_pm(wv)
    out["wo_h"] = regroup(tile_pm(wo), 16, 8, 256)
    wgT = np.ascontiguousarray(wg.T)
    wuT = np.ascontiguousarray(wu.T)
    wdT = np.ascontiguousarray(wd.T)
    out["wg_h"] = regroup(tile_pm(wgT), 16, 32, 256)
    out["wu_h"] = regroup(tile_pm(wuT), 16, 32, 256)
    out["wd_h"] = regroup(tile_pm(wdT), 64, 8, 256)
    out["id_h"] = np.eye(128, dtype=ml_dtypes.bfloat16)
    return out


def make_xt(xb, half):
    other = 1 - half
    xperm = np.concatenate([xb[other * 2048:(other + 1) * 2048],
                            xb[half * 2048:(half + 1) * 2048]], axis=0)
    xT = np.ascontiguousarray(xperm.T)
    return regroup(tile_pm(xT), 16, 16, 256)


def kernel(L0_attn_Q_input, L0_attn_Wq, L0_attn_Wk, L0_attn_Wv, L0_attn_Wo,
           L0_ffn_w_gate, L0_ffn_w_up, L0_ffn_w_down):
    from concourse.bass_utils import run_bass_kernel_spmd

    x = np.ascontiguousarray(np.asarray(L0_attn_Q_input, dtype=np.float32))
    wq = np.ascontiguousarray(np.asarray(L0_attn_Wq, dtype=np.float32))
    wk = np.ascontiguousarray(np.asarray(L0_attn_Wk, dtype=np.float32))
    wv = np.ascontiguousarray(np.asarray(L0_attn_Wv, dtype=np.float32))
    wo = np.ascontiguousarray(np.asarray(L0_attn_Wo, dtype=np.float32))
    wg = np.ascontiguousarray(np.asarray(L0_ffn_w_gate, dtype=np.float32))
    wu = np.ascontiguousarray(np.asarray(L0_ffn_w_up, dtype=np.float32))
    wd = np.ascontiguousarray(np.asarray(L0_ffn_w_down, dtype=np.float32))

    shared = make_core_inputs(x, wq, wk, wv, wo, wg, wu, wd)
    in_maps = []
    for core in range(8):
        b, half = core // 2, core % 2
        m = dict(shared)
        m["xt_h"] = make_xt(x[b], half)
        m["xn_h"] = np.ascontiguousarray(x[b, half * 2048:(half + 1) * 2048])
        in_maps.append(m)

    nc = build(upto="full", debug=False)
    res = run_bass_kernel_spmd(nc, in_maps, core_ids=list(range(8)))

    out = np.empty((4, 4096, 2048), dtype=np.float32)
    for core in range(8):
        b, half = core // 2, core % 2
        out[b, half * 2048:(half + 1) * 2048] = np.asarray(
            res.results[core]["y_h"], dtype=np.float32)
    return out


# version 03
# speedup vs baseline: 1.3612x; 1.0888x over previous
import numpy as np
import ml_dtypes
import concourse.bass as bass
import concourse.mybir as mybir
from contextlib import ExitStack

F32 = mybir.dt.float32
BF16 = mybir.dt.bfloat16
AF = mybir.ActivationFunctionType
ALU = mybir.AluOpType
AX = mybir.AxisListType
AP = bass.AP

D = 2048
DT = 16
H = 16
SF = 4096
NHC = 16
T = 2048
NQC = 4
HID = 8192
HTN = 64
NG = 32
NDC = 8
SCALE = float(1.0 / np.sqrt(128.0))
EPS = 1e-5
RMEAN = float(1.0 / 2048.0)
RSQ = float(1.0 / np.sqrt(2048.0))

ENGS = ("tensor", "scalar", "vector", "sync", "gpsimd")


class Rec:
    def __init__(self):
        self.ops = {k: [] for k in ENGS}
        self.cnt = {}

    def w(self, eng, sem, val):
        if val > 0:
            self.ops[eng].append(("w", sem, val))

    def do(self, eng, fn, sem=None, amt=1):
        if sem is not None:
            self.cnt[sem] = self.cnt.get(sem, 0) + amt
        self.ops[eng].append(("c", fn, sem, amt))

    def c(self, sem):
        return self.cnt.get(sem, 0)

    def scope(self, name):
        for eng in ENGS:
            self.ops[eng].append(("s", name))

    def replay(self, eng, e, sems, nc=None):
        cur = None
        for item in self.ops[eng]:
            if item[0] == "w":
                e.wait_ge(sems[item[1]], item[2])
            elif item[0] == "s":
                if nc is not None:
                    if cur is not None:
                        nc.leave_named_scope(cur[0], cur[1], False)
                    sid, _ = nc.enter_named_scope(item[1], False)
                    cur = (item[1], sid)
            else:
                _, fn, sem, amt = item
                ins = fn(e)
                if sem is not None:
                    ins.then_inc(sems[sem], amt)
        if cur is not None and nc is not None:
            nc.leave_named_scope(cur[0], cur[1], False)


SEM_NAMES = [
    "d_wkv", "d_wq", "d_xt0", "d_xt1", "d_wo0", "d_wo1", "d_wg0", "d_wg1",
    "d_wd0", "d_wd1", "d_id", "d_xn0", "d_xn1", "d_xn2", "d_xn3",
    "d_y1f0", "d_y1f1",
    "d_y1b0", "d_y1b1", "d_y1T0", "d_y1T1", "d_y1T2", "d_y1T3", "d_gw0", "d_gw1", "d_gr0", "d_gr1",
    "d_y1c0", "d_y1c1", "d_y1c2", "d_y1c3", "d_r2w0", "d_r2w1", "d_r2w2",
    "d_r2w3", "d_r2r0", "d_r2r1", "d_out0", "d_out1",
    "s_init", "s_Ak", "s_Akf", "s_Av", "s_Avt", "s_Atr", "s_Avn",
    "s_Aq", "s_Aqf",
    "s_st", "s_exp", "s_ho", "s_sums", "s_recip", "s_rep", "s_repc", "s_hoevac",
    "s_att", "s_r1v", "s_stat", "s_nvpe", "s_std", "s_y1f", "s_y1b",
    "s_gate", "s_up", "s_silu", "s_gated", "sD1_g",
    "s_down", "s_r2v", "s_stat2", "s_nvpe2", "s_std2", "s_y2", "s_tr", "s_trf",
    "s_sumc", "d_wdc0", "d_wdc1", "d_wds0", "d_wds1", "s_rst", "s_jk",
]


def build(upto="full", debug=False):
    nc = bass.Bass("TRN2", target_bir_lowering=False)

    xt_h = nc.dram_tensor("xt_h", [128, NHC * DT * 256], F32, kind="ExternalInput")
    xn_h = nc.dram_tensor("xn_h", [T, D], F32, kind="ExternalInput")
    wq_h = nc.dram_tensor("wq_h", [128, DT * D], F32, kind="ExternalInput")
    wk_h = nc.dram_tensor("wk_h", [128, DT * 128], F32, kind="ExternalInput")
    wv_h = nc.dram_tensor("wv_h", [128, DT * 128], F32, kind="ExternalInput")
    wo_h = nc.dram_tensor("wo_h", [128, 8 * 16 * 256], F32, kind="ExternalInput")
    wg_h = nc.dram_tensor("wg_h", [128, NG * DT * 256], F32, kind="ExternalInput")
    wu_h = nc.dram_tensor("wu_h", [128, NG * DT * 256], F32, kind="ExternalInput")
    wd_h = nc.dram_tensor("wd_h", [128, NDC * HTN * 256], F32, kind="ExternalInput")
    id_h = nc.dram_tensor("id_h", [128, 128], BF16, kind="ExternalInput")

    skind = "ExternalOutput" if debug else "Internal"
    y1f_hbm = nc.dram_tensor("y1f_hbm", [T, D], F32, kind=skind)
    y1b_hbm = nc.dram_tensor("y1b_hbm", [T, D], BF16, kind=skind)
    gat_hbm = nc.dram_tensor("gat_hbm", [8 * HTN, 128, 256], BF16, kind=skind)
    r2_hbm = nc.dram_tensor("r2_hbm", [T, 7 * 256], F32, kind="Internal")
    wdb_hbm = nc.dram_tensor("wdb_hbm", [NDC, 128, HTN * 256], BF16, kind="Internal")
    y_h = nc.dram_tensor("y_h", [T, D], F32, kind="ExternalOutput")

    dbg = {}
    if debug:
        for nm, shp, dt in [("kT", [128, SF], BF16), ("vN", [128, SF], BF16),
                            ("qT", [128, H * T], BF16), ("ho", [128, 8192], BF16),
                            ("r1", [512, D], F32)]:
            dbg[nm] = nc.dram_tensor("dbg_" + nm, shp, dt, kind="ExternalOutput")

    with ExitStack() as ctx:
        E = ctx.enter_context
        RA = E(nc.sbuf_tensor("RA", [128, 32768], BF16))
        RB = E(nc.sbuf_tensor("RB", [128, 32768], BF16))
        RC = E(nc.sbuf_tensor("RC", [128, 8192], BF16))
        RD = E(nc.sbuf_tensor("RD", [128, 16384], BF16))
        RE = E(nc.sbuf_tensor("RE", [128, 8192], BF16))
        PTB = E(nc.sbuf_tensor("PTB", [128, 3072], BF16))
        Y1B = E(nc.sbuf_tensor("Y1B", [128, 4096], BF16))
        IDN = E(nc.sbuf_tensor("IDN", [128, 128], BF16))
        ONE = E(nc.sbuf_tensor("ONE", [128, 1], BF16))
        ONF = E(nc.sbuf_tensor("ONF", [1, 128], F32))
        LNS = E(nc.sbuf_tensor("LNS", [128, 24], F32))

        stA = E(nc.psum_tensor("stA", [128, 1024], F32))
        stB = E(nc.psum_tensor("stB", [128, 1024], F32))
        ho0 = E(nc.psum_tensor("ho0", [128, 512], F32))
        ho1 = E(nc.psum_tensor("ho1", [128, 512], F32))
        sr0 = E(nc.psum_tensor("sr0", [128, 512], F32))
        sr1 = E(nc.psum_tensor("sr1", [128, 512], F32))

        sems = {n: E(nc.semaphore(n)) for n in SEM_NAMES}

        r = Rec()
        _emit(r, upto, debug, dict(
            xt_h=xt_h, xn_h=xn_h, wq_h=wq_h, wk_h=wk_h, wv_h=wv_h, wo_h=wo_h,
            wg_h=wg_h, wu_h=wu_h, wd_h=wd_h, id_h=id_h,
            y1f_hbm=y1f_hbm, y1b_hbm=y1b_hbm, gat_hbm=gat_hbm, r2_hbm=r2_hbm,
            wdb_hbm=wdb_hbm,
            y_h=y_h, dbg=dbg,
            RA=RA, RB=RB, RC=RC, RD=RD, RE=RE, PTB=PTB, Y1B=Y1B, IDN=IDN,
            ONE=ONE, ONF=ONF, LNS=LNS,
            stA=stA, stB=stB, ho0=ho0, ho1=ho1, sr0=sr0, sr1=sr1))

        block = E(nc.Block())

        @block.tensor
        def _(e):
            r.replay("tensor", e, sems, nc)

        @block.scalar
        def _(e):
            r.replay("scalar", e, sems, nc)

        @block.vector
        def _(e):
            r.replay("vector", e, sems, nc)

        @block.sync
        def _(e):
            r.replay("sync", e, sems, nc)

        @block.gpsimd
        def _(e):
            r.replay("gpsimd", e, sems, nc)

    return nc


def _emit(r, upto, debug, t):
    xt_h, xn_h = t["xt_h"], t["xn_h"]
    wq_h, wk_h, wv_h, wo_h = t["wq_h"], t["wk_h"], t["wv_h"], t["wo_h"]
    wg_h, wu_h, wd_h, id_h = t["wg_h"], t["wu_h"], t["wd_h"], t["id_h"]
    y1f_hbm, y1b_hbm = t["y1f_hbm"], t["y1b_hbm"]
    gat_hbm, r2_hbm, y_h = t["gat_hbm"], t["r2_hbm"], t["y_h"]
    wdb_hbm = t["wdb_hbm"]
    dbg = t["dbg"]
    RA, RB, RC, RD, RE = t["RA"], t["RB"], t["RC"], t["RD"], t["RE"]
    PTB, Y1B, IDN, ONE, ONF = t["PTB"], t["Y1B"], t["IDN"], t["ONE"], t["ONF"]
    LNS = t["LNS"]
    stA, stB = t["stA"], t["stB"]
    ho0, ho1, sr0, sr1 = t["ho0"], t["ho1"], t["sr0"], t["sr1"]

    RAf = RA[:, :].bitcast(F32)
    RCf = RC[:, :].bitcast(F32)
    RDf = RD[:, :].bitcast(F32)
    REf = RE[:, :].bitcast(F32)

    def kT(a, b):
        return RE[:, a:b]

    def vN(a, b):
        return RE[:, 4096 + a:4096 + b]

    def r1_ap(qt, a, b):
        return RAf[:, qt * 2048 + a: qt * 2048 + b]

    def y1f_ap(par):
        return RAf[:, 8192 + par * 2048: 8192 + (par + 1) * 2048]

    def rep_ap(par):
        return RAf[:, 13312 + par * 512: 13312 + (par + 1) * 512]

    def xnt_ap(par):
        return RAf[:, 12288 + par * 256: 12288 + (par + 1) * 256]

    def rcp_ap(par):
        return RAf[0:1, 14336 + par * 512: 14336 + (par + 1) * 512]

    JK1 = RAf[:, 15360:15616]
    JK2 = RDf[:, 4096:4352]

    def xt_ap(par, a, b):
        return RC[:, par * 4096 + a: par * 4096 + b]

    def wo_ap(par, a, b):
        return RC[:, par * 4096 + a: par * 4096 + b]

    wk_sb = RD[:, 0:2048]
    wv_sb = RD[:, 2048:4096]

    def vt_ap(par):
        return RD[:, 4096 + par * 256: 4096 + (par + 1) * 256]

    def hoT_ap(par, a, b):
        return RD[:, par * 8192 + a: par * 8192 + b]

    def wg_ap(par, a, b):
        return RD[:, par * 4096 + a: par * 4096 + b]

    def wu_ap(par, a, b):
        return RC[:, par * 4096 + a: par * 4096 + b]

    def gstage_ap(j, a, b):
        return RE[:, j * 2048 + a: j * 2048 + b]

    def silu_ap(par):
        return REf[:, 2048 + par * 512: 2048 + (par + 1) * 512]

    def y1c_ap(par):
        return REf[:, par * 256: (par + 1) * 256]

    def r2c_ap(par):
        return REf[:, 1024 + par * 256: 1024 + (par + 1) * 256]

    def r2f_ap(par, a, b):
        return RCf[:, par * 2048 + a: par * 2048 + b]

    def y2_ap(par):
        return RDf[:, par * 2048: (par + 1) * 2048]

    def gat_sb(par, a, b):
        return RB[:, par * 16384 + a: par * 16384 + b]

    def wd_sb(par, a, b):
        return RA[:, par * 16384 + a: par * 16384 + b]

    mI4 = lambda i: LNS[:, i:i + 1]
    e24 = lambda i: LNS[:, 4 + i:5 + i]
    nv4 = lambda i: LNS[:, 8 + i:9 + i]
    sd4 = lambda i: LNS[:, 12 + i:13 + i]
    rs4 = lambda i: LNS[:, 16 + i:17 + i]

    r.scope("phA")
    r.do("gpsimd", lambda e: e.dma_start(wk_sb, wk_h[:, :]), "d_wkv", 16)
    r.do("gpsimd", lambda e: e.dma_start(wv_sb, wv_h[:, :]), "d_wkv", 16)
    r.do("gpsimd", lambda e: e.dma_start(RA[:, :], wq_h[:, :]), "d_wq", 16)
    r.do("sync", lambda e: e.dma_start(IDN[:, :], id_h[:, :]), "d_id", 16)
    r.do("vector", lambda e: e.memset(ONE[:, :], 1.0))
    r.do("vector", lambda e: e.memset(LNS[:, 20:21], EPS))
    r.do("vector", lambda e: e.memset(ONF[:, :], 1.0), "s_init", 1)

    for c in range(NHC):
        par = c % 2
        dsem = "d_xt%d" % par
        if c >= 2:
            cp = c - 2
            if cp < 8:
                r.w("gpsimd", "s_Av", cp + 1)
            else:
                r.w("gpsimd", "s_Aq", (cp - 8 + 1) * DT)
        r.do("gpsimd",
             (lambda c=c, par=par: (lambda e: e.dma_start(
                 xt_ap(par, 0, 4096), xt_h[:, c * 4096:(c + 1) * 4096])))(),
             dsem, 16)

        r.w("tensor", dsem, 16 * (c // 2 + 1))
        if c == 0:
            r.w("tensor", "d_wkv", 32)
            r.w("tensor", "d_id", 16)
        if c >= 1:
            r.w("tensor", "s_Akf", r.c("s_Akf"))
        for i in range(DT):
            r.do("tensor",
                 (lambda i=i, par=par: (lambda e: e.matmul(
                     stA[:, 512:768], wk_sb[:, i * 128:(i + 1) * 128],
                     xt_ap(par, i * 256, (i + 1) * 256),
                     start=(i == 0), stop=(i == DT - 1))))(),
                 "s_Ak" if i == DT - 1 else None)
        if c >= 1:
            r.w("tensor", "s_Avt", r.c("s_Avt"))
        for i in range(DT):
            r.do("tensor",
                 (lambda i=i, par=par: (lambda e: e.matmul(
                     stB[:, 512:768], wv_sb[:, i * 128:(i + 1) * 128],
                     xt_ap(par, i * 256, (i + 1) * 256),
                     start=(i == 0), stop=(i == DT - 1))))(),
                 "s_Av" if i == DT - 1 else None)
        r.w("vector", "s_Ak", r.c("s_Ak"))
        r.do("vector",
             (lambda c=c: (lambda e: e.tensor_copy(
                 kT(c * 256, (c + 1) * 256), stA[:, 512:768])))(),
             "s_Akf", 1)
        r.w("vector", "s_Av", r.c("s_Av"))
        if c >= 2:
            r.w("vector", "s_Atr", c - 1)
        r.do("vector",
             (lambda par=par: (lambda e: e.tensor_copy(vt_ap(par), stB[:, 512:768])))(),
             "s_Avt", 1)
        r.w("tensor", "s_Avt", r.c("s_Avt"))
        if c >= 2:
            r.w("tensor", "s_Avn", r.c("s_Avn") - 1)
        hob = (ho0, ho1)[par]
        for j in range(2):
            r.do("tensor",
                 (lambda par=par, j=j, hob=hob: (lambda e: e.transpose(
                     hob[:, :].bitcast(BF16)[:, j * 128:(j + 1) * 128],
                     vt_ap(par)[:, j * 128:(j + 1) * 128], IDN[:, :])))(),
                 "s_Atr" if j == 1 else None)
        r.w("vector", "s_Atr", r.c("s_Atr"))
        r.do("vector",
             (lambda c=c, hob=hob: (lambda e: e.tensor_copy(
                 vN(c * 256, (c + 1) * 256), hob[:, :].bitcast(BF16)[:, 0:256])))(),
             "s_Avn", 1)
        if c >= 8:
            if c == 8:
                r.w("tensor", "d_wq", 16)
            for f in range(DT):
                gq = (c - 8) * DT + f
                qb = (stA, stB)[gq % 2]
                r.w("tensor", "s_Aqf", gq - 1)
                for i in range(DT):
                    r.do("tensor",
                         (lambda i=i, f=f, par=par, qb=qb: (lambda e: e.matmul(
                             qb[:, 0:256],
                             RA[:, i * 2048 + f * 128: i * 2048 + (f + 1) * 128],
                             xt_ap(par, i * 256, (i + 1) * 256),
                             start=(i == 0), stop=(i == DT - 1))))(),
                         "s_Aq" if i == DT - 1 else None)
                r.w("vector", "s_Aq", r.c("s_Aq"))
                r.do("vector",
                     (lambda c=c, f=f, qb=qb: (lambda e: e.tensor_copy(
                         RB[:, f * 2048 + (c - 8) * 256: f * 2048 + (c - 7) * 256],
                         qb[:, 0:256])))(),
                     "s_Aqf", 1)

    if upto == "A":
        r.w("sync", "s_Akf", NHC)
        r.w("sync", "s_Avn", NHC)
        r.w("sync", "s_Aqf", 128)
        r.do("sync", lambda e: e.dma_start(dbg["kT"][:, :], kT(0, 4096)), "d_out0", 16)
        r.do("sync", lambda e: e.dma_start(dbg["vN"][:, :], vN(0, 4096)), "d_out0", 16)
        r.do("sync", lambda e: e.dma_start(dbg["qT"][:, :], RB[:, :]), "d_out0", 16)
        r.w("sync", "d_out0", 48)
        return

    r.scope("phBC")
    r.w("tensor", "s_Akf", NHC)
    r.w("tensor", "s_Avn", NHC)
    r.w("tensor", "s_Aqf", 128)
    r.w("tensor", "s_init", 1)

    def emit_rep(idx):
        srb = (sr0, sr1)[idx % 2]
        qc_, h_ = idx // 16, idx % 16
        r.w("tensor", "s_recip", idx + 1)
        r.do("tensor",
             (lambda idx=idx, srb=srb: (lambda e: e.matmul(
                 srb[:, :], ONF[:, :],
                 rcp_ap(idx % 2),
                 start=True, stop=True)))(),
             "s_rep", 1)
        r.w("vector", "s_repc", idx + 1)
        r.w("vector", "s_ho", idx + 1)
        if qc_ >= 2:
            r.w("vector", "s_att", 32 * (qc_ - 1))
        r.do("vector",
             (lambda idx=idx, qc_=qc_, h_=h_: (lambda e: e.tensor_tensor(
                 hoT_ap(qc_ % 2, h_ * 512, (h_ + 1) * 512),
                 (ho0, ho1)[idx % 2][:, :], rep_ap(idx % 2), ALU.mult)))(),
             "s_hoevac", 1)

    def emit_repc(idx):
        srb = (sr0, sr1)[idx % 2]
        r.w("scalar", "s_rep", idx + 1)
        r.w("scalar", "s_hoevac", idx - 1)
        r.do("scalar",
             (lambda idx=idx, srb=srb: (lambda e: e.copy(rep_ap(idx % 2), srb[:, :])))(),
             "s_repc", 1)

    def emit_pv_pair(idx, p, need_wait=True):
        gp = idx * 16 + p
        sl = gp % 3
        if need_wait:
            r.w("tensor", "s_exp", gp + 1)
        hob = (ho0, ho1)[idx % 2]
        srb = (sr0, sr1)[idx % 2]
        if p == 0 and idx >= 2:
            r.w("tensor", "s_hoevac", idx - 1)
            r.w("tensor", "s_repc", idx - 1)
        for sub in range(2):
            tt = 2 * p + sub
            r.do("tensor",
                 (lambda tt=tt, sl=sl, sub=sub, hob=hob: (lambda e: e.matmul(
                     hob[:, :], vN(tt * 128, (tt + 1) * 128),
                     PTB[:, sl * 1024 + sub * 512: sl * 1024 + (sub + 1) * 512],
                     start=(tt == 0), stop=(tt == 31))))(),
                 "s_ho" if tt == 31 else None)
        for sub in range(2):
            tt = 2 * p + sub
            r.do("tensor",
                 (lambda tt=tt, sl=sl, sub=sub, srb=srb: (lambda e: e.matmul(
                     srb[0:1, :], ONE[:, :],
                     PTB[:, sl * 1024 + sub * 512: sl * 1024 + (sub + 1) * 512],
                     start=(tt == 0), stop=(tt == 31))))(),
                 "s_sums" if tt == 31 else None)

    for qc in range(NQC):
        if qc == 0:
            r.w("gpsimd", "s_Aq", 128)
        for dc in range(8):
            gw = qc * 8 + dc
            if gw >= 2:
                gp_ = gw - 2
                r.w("gpsimd", "s_att", (gp_ // 8) * 32 + ((gp_ % 8) + 1) * 4)
            r.do("gpsimd",
                 (lambda dc=dc, gw=gw: (lambda e: e.dma_start(
                     wo_ap(gw % 2, 0, 4096), wo_h[:, dc * 4096:(dc + 1) * 4096])))(),
                 "d_wo%d" % (gw % 2), 16)
            for qt in range(4):
                gxn = qc * 32 + dc * 4 + qt
                r.w("gpsimd", "s_r1v", gxn - 3)
                r.do("gpsimd",
                     (lambda qc=qc, dc=dc, qt=qt, gxn=gxn: (lambda e: e.dma_start(
                         xnt_ap(gxn % 4),
                         xn_h[qc * 512 + qt * 128: qc * 512 + (qt + 1) * 128,
                              dc * 256:(dc + 1) * 256])))(),
                     "d_xn%d" % (gxn % 4), 16)
        if qc >= 1:
            r.w("tensor", "s_r1v", 32 * qc)
        for h in range(H):
            idx = qc * 16 + h
            for p in range(16):
                gp = idx * 16 + p
                stb = (stA, stB)[p % 2]
                r.w("tensor", "s_exp", gp - 1)
                for sub in range(2):
                    tt = 2 * p + sub
                    r.do("tensor",
                         (lambda tt=tt, h=h, qc=qc, stb=stb, sub=sub: (lambda e: e.matmul(
                             stb[:, sub * 512:(sub + 1) * 512],
                             kT(tt * 128, (tt + 1) * 128),
                             RB[:, h * 2048 + qc * 512: h * 2048 + (qc + 1) * 512],
                             start=True, stop=True)))(),
                         "s_st" if sub == 1 else None)
                if p == 3 and idx >= 1 and (idx % 16) != 0:
                    emit_rep(idx - 1)
                if p >= 2:
                    emit_pv_pair(idx, p - 2, need_wait=False)
            emit_pv_pair(idx, 14)
            emit_pv_pair(idx, 15)
            for p in range(16):
                gp = idx * 16 + p
                stb = (stA, stB)[p % 2]
                r.w("scalar", "s_st", gp + 1)
                r.do("scalar",
                     (lambda gp=gp, stb=stb: (lambda e: e.activation(
                         PTB[:, (gp % 3) * 1024:(gp % 3 + 1) * 1024], stb[:, :],
                         AF.Exp, scale=SCALE)))(),
                     "s_exp", 1)
                if p == 4 and idx >= 1 and (idx % 16) != 0:
                    emit_repc(idx - 1)
            r.w("vector", "s_sums", idx + 1)
            r.w("vector", "s_rep", idx - 1)
            r.do("vector",
                 (lambda idx=idx: (lambda e: e.reciprocal(
                     rcp_ap(idx % 2),
                     (sr0, sr1)[idx % 2][0:1, :])))(),
                 "s_recip", 1)

        emit_rep(qc * 16 + 15)
        emit_repc(qc * 16 + 15)
        r.w("tensor", "s_exp", (qc + 1) * 256)
        r.w("tensor", "s_hoevac", (qc + 1) * 16)
        attrot = [stA[:, 0:256], stA[:, 512:768], stB[:, 0:256], stB[:, 512:768]]
        for dc in range(8):
            gw = qc * 8 + dc
            r.w("tensor", "d_wo%d" % (gw % 2), 16 * (gw // 2 + 1))
            for qt in range(4):
                g_att = qc * 32 + dc * 4 + qt
                ab = attrot[g_att % 4]
                r.w("tensor", "s_r1v", g_att - 3)
                for f in range(16):
                    r.do("tensor",
                         (lambda f=f, qt=qt, qc=qc, gw=gw, ab=ab: (lambda e: e.matmul(
                             ab,
                             hoT_ap(qc % 2, f * 512 + qt * 128, f * 512 + (qt + 1) * 128),
                             wo_ap(gw % 2, f * 256, (f + 1) * 256),
                             start=(f == 0), stop=(f == 15))))(),
                         "s_att" if f == 15 else None)
                r.w("vector", "s_att", g_att + 1)
                r.w("vector", "d_xn%d" % (g_att % 4), 16 * (g_att // 4 + 1))
                r.do("vector",
                     (lambda qt=qt, dc=dc, g_att=g_att, ab=ab: (lambda e: e.tensor_tensor(
                         r1_ap(qt, dc * 256, (dc + 1) * 256), ab,
                         xnt_ap(g_att % 4), ALU.add)))(),
                     "s_r1v", 1)
        if debug and upto == "C" and qc == 0:
            r.w("sync", "s_hoevac", 16)
            r.do("sync", lambda e: e.dma_start(dbg["ho"][:, :], RD[:, 0:8192]),
                 "d_out0", 16)
            r.w("sync", "s_r1v", 32)
            for qt in range(4):
                r.do("sync",
                     (lambda qt=qt: (lambda e: e.dma_start(
                         dbg["r1"][qt * 128:(qt + 1) * 128, :],
                         r1_ap(qt, 0, 2048))))(), "d_out0", 16)
        for qt in range(4):
            gqt = qc * 4 + qt
            r.w("scalar", "s_r1v", qc * 32 + 29 + qt)
            if gqt >= 4:
                r.w("scalar", "s_y1f", gqt - 3)
            r.do("scalar",
                 (lambda qt=qt: (lambda e: e.activation(
                     PTB[:, 0:2048], r1_ap(qt, 0, 2048), AF.Copy, scale=RMEAN,
                     accum_out=mI4(qt))))(),
                 "s_stat", 1)
            r.w("scalar", "s_stat", r.c("s_stat"))
            r.do("scalar",
                 (lambda qt=qt: (lambda e: e.activation(
                     PTB[:, 0:2048], r1_ap(qt, 0, 2048), AF.Square, scale=RSQ,
                     accum_out=e24(qt))))(),
                 "s_stat", 1)
            r.w("vector", "s_stat", 2 * (gqt + 1))
            r.do("vector",
                 (lambda qt=qt: (lambda e: e.scalar_tensor_tensor(
                     nv4(qt), mI4(qt), mI4(qt), e24(qt), ALU.mult, ALU.subtract)))(),
                 "s_nvpe", 1)
            r.w("scalar", "s_nvpe", gqt + 1)
            r.do("scalar",
                 (lambda qt=qt: (lambda e: e.activation(
                     sd4(qt), nv4(qt), AF.Sqrt, scale=-1.0,
                     bias=LNS[:, 20:21])))(),
                 "s_std", 1)
            r.w("vector", "s_std", gqt + 1)
            r.do("vector",
                 (lambda qt=qt: (lambda e: e.reciprocal(rs4(qt), sd4(qt))))(),
                 "s_rst", 1)
            for _ in range(3):
                r.w("vector", "s_jk", r.c("s_jk"))
                r.do("vector", lambda e: e.tensor_copy(JK1, JK1), "s_jk", 1)
            if gqt >= 2:
                r.w("vector", "d_y1f%d" % (gqt % 2), 16 * (gqt // 2))
            r.w("vector", "s_rst", r.c("s_rst"))
            r.w("vector", "s_r1v", r.c("s_r1v"))
            r.do("vector",
                 (lambda qt=qt, gqt=gqt: (lambda e: e.tensor_scalar(
                     y1f_ap(gqt % 2), r1_ap(qt, 0, 2048), mI4(qt), rs4(qt),
                     ALU.subtract, ALU.mult)))(),
                 "s_y1f", 1)
            if gqt >= 2:
                r.w("vector", "d_y1b%d" % (gqt % 2), 16 * (gqt // 2))
            r.w("vector", "s_y1f", r.c("s_y1f"))
            r.do("vector",
                 (lambda gqt=gqt: (lambda e: e.tensor_copy(
                     Y1B[:, (gqt % 2) * 2048:(gqt % 2 + 1) * 2048], y1f_ap(gqt % 2))))(),
                 "s_y1b", 1)
            r.w("sync", "s_y1f", gqt + 1)
            r.do("sync",
                 (lambda gqt=gqt: (lambda e: e.dma_start(
                     y1f_hbm[gqt * 128:(gqt + 1) * 128, :], y1f_ap(gqt % 2))))(),
                 "d_y1f%d" % (gqt % 2), 16)
            r.w("sync", "s_y1b", gqt + 1)
            r.do("sync",
                 (lambda gqt=gqt: (lambda e: e.dma_start(
                     y1b_hbm[gqt * 128:(gqt + 1) * 128, :],
                     Y1B[:, (gqt % 2) * 2048:(gqt % 2 + 1) * 2048])))(),
                 "d_y1b%d" % (gqt % 2), 16)
        tq = "scalar" if qc == 3 else "sync"
        r.w(tq, "d_y1b0", 16 * 2 * (qc + 1))
        r.w(tq, "d_y1b1", 16 * 2 * (qc + 1))
        r.w(tq, "s_st", (qc + 1) * 256)
        for i in range(DT):
            r.do(tq,
                 (lambda i=i, qc=qc: (lambda e: e.dma_start(
                     RB[:, i * 2048 + qc * 512: i * 2048 + (qc + 1) * 512],
                     y1b_hbm[qc * 512:(qc + 1) * 512, i * 128:(i + 1) * 128],
                     transpose=True)))(),
                 "d_y1T%d" % qc, 16)

    if upto == "C":
        r.w("sync", "d_y1f0", 16 * 8)
        r.w("sync", "d_y1f1", 16 * 8)
        return

    r.scope("phD1")

    for g in range(NG):
        if g == 0:
            r.w("gpsimd", "s_att", 96)
        elif g == 1:
            r.w("gpsimd", "s_att", 128)
        else:
            r.w("gpsimd", "sD1_g", g - 1)
        r.do("gpsimd",
             (lambda g=g: (lambda e: e.dma_start(
                 wg_ap(g % 2, 0, 4096), wg_h[:, g * 4096:(g + 1) * 4096])))(),
             "d_wg%d" % (g % 2), 16)
        if g == 0:
            r.w("gpsimd", "s_att", 124)
        r.do("gpsimd",
             (lambda g=g: (lambda e: e.dma_start(
                 wu_ap(g % 2, 0, 4096), wu_h[:, g * 4096:(g + 1) * 4096])))(),
             "d_wg%d" % (g % 2), 16)
        if g == 3:
            r.w("gpsimd", "s_stat", 32)
            r.w("gpsimd", "s_y1f", 16)
            r.w("gpsimd", "s_r1v", 128)
            r.w("gpsimd", "s_hoevac", 64)
            r.w("gpsimd", "d_y1f0", 16 * 8)
            r.w("gpsimd", "d_y1f1", 16 * 8)
            r.do("gpsimd", lambda e: e.dma_start(
                wd_sb(0, 0, 16384), wd_h[:, 0:16384]), "d_wd0", 16)
        if g == 16:
            r.do("gpsimd", lambda e: e.dma_start(
                wd_sb(1, 0, 16384), wd_h[:, 16384:32768]), "d_wd1", 16)

    nsil = 0
    ngat = 0
    for g in range(NG):
        if g == 0:
            r.w("tensor", "s_r1v", 128)
        r.w("tensor", "d_wg%d" % (g % 2), 32 * (g // 2 + 1))
        for j in range(2):
            ht = 2 * g + j
            for tc in range(4):
                sidx = ht * 4 + tc
                gb_off = (tc % 2) * 512
                if g == 0:
                    r.w("tensor", "d_y1T%d" % tc, 16 * 16)
                r.w("tensor", "s_silu", sidx - 1)
                for i in range(DT):
                    r.do("tensor",
                         (lambda i=i, j=j, g=g, tc=tc, gb_off=gb_off: (lambda e: e.matmul(
                             stA[:, gb_off:gb_off + 512],
                             wg_ap(g % 2, i * 256 + j * 128, i * 256 + (j + 1) * 128),
                             RB[:, i * 2048 + tc * 512: i * 2048 + (tc + 1) * 512],
                             start=(i == 0), stop=(i == DT - 1))))(),
                         "s_gate" if i == DT - 1 else None)
                r.w("tensor", "s_gated", sidx - 1)
                for i in range(DT):
                    r.do("tensor",
                         (lambda i=i, j=j, g=g, tc=tc, gb_off=gb_off: (lambda e: e.matmul(
                             stB[:, gb_off:gb_off + 512],
                             wu_ap(g % 2, i * 256 + j * 128, i * 256 + (j + 1) * 128),
                             RB[:, i * 2048 + tc * 512: i * 2048 + (tc + 1) * 512],
                             start=(i == 0), stop=(i == DT - 1))))(),
                         "s_up" if i == DT - 1 else None)
                nsil += 1
                r.w("scalar", "s_gate", nsil)
                r.w("scalar", "s_gated", nsil - 2)
                r.do("scalar",
                     (lambda tc=tc, gb_off=gb_off: (lambda e: e.activation(
                         silu_ap(tc % 2), stA[:, gb_off:gb_off + 512], AF.Silu)))(),
                     "s_silu", 1)
                ngat += 1
                r.w("vector", "s_silu", ngat)
                r.w("vector", "s_up", ngat)
                if g >= 1 and j == 0 and tc == 0:
                    r.w("vector", "d_gw%d" % ((g - 1) % 2), 16 * 8 * ((g - 1) // 2 + 1))
                r.do("vector",
                     (lambda j=j, tc=tc, gb_off=gb_off: (lambda e: e.tensor_tensor(
                         gstage_ap(j, tc * 512, (tc + 1) * 512),
                         silu_ap(tc % 2), stB[:, gb_off:gb_off + 512], ALU.mult)))(),
                     "s_gated", 1)
        r.do("tensor", (lambda: (lambda e: e.matmul(
            sr0[0:1, 0:1], ONE[:, :], ONE[:, 0:1], start=True, stop=True)))(),
            "sD1_g", 1)
        r.w("sync", "s_gated", 8 * (g + 1))
        for tc2 in range(8):
            r.do("sync",
                 (lambda g=g, tc2=tc2: (lambda e: e.dma_start(
                     AP(gat_hbm, (tc2 * HTN + 2 * g) * 32768,
                        [[256, 128], [32768, 2], [1, 256]]),
                     AP(RE, tc2 * 256, [[8192, 128], [2048, 2], [1, 256]]))))(),
                 "d_gw%d" % (g % 2), 16)

    if upto == "D1":
        r.w("sync", "d_gw0", 16 * 8 * 16)
        r.w("sync", "d_gw1", 16 * 8 * 16)
        return

    r.scope("phD2")
    r.w("tensor", "s_hoevac", 64)
    r.w("tensor", "s_repc", 64)
    r.w("tensor", "s_silu", 256)
    r.w("tensor", "s_gated", 256)
    rot = [ho0[:, 0:256], ho1[:, 0:256], sr0[:, 0:256], sr1[:, 0:256],
           stA[:, 0:256], stB[:, 0:256]]
    gat_loads = []

    sched = []
    for dc in range(NDC):
        tc2s = list(range(8)) if dc % 2 == 0 else list(range(7, -1, -1))
        for tci, tc2 in enumerate(tc2s):
            reuse = (dc > 0 and tci == 0)
            sched.append((dc, tci, tc2, reuse))

    def pos2dcytt(j):
        pg, sub = divmod(j, 2)
        dcq, r8 = divmod(pg, 8)
        tc2j = (list(range(8)) if dcq % 2 == 0 else list(range(7, -1, -1)))[r8]
        return dcq, tc2j * 2 + sub

    for j in range(3):
        dcq, ttj = pos2dcytt(j)
        if j == 0:
            r.w("scalar", "d_gw0", 16 * 8 * 16)
            r.w("scalar", "d_gw1", 16 * 8 * 16)
        r.do("scalar",
             (lambda dcq=dcq, ttj=ttj, j=j: (lambda e: e.dma_start(
                 y1c_ap(j % 4),
                 y1f_hbm[ttj * 128:(ttj + 1) * 128,
                         dcq * 256:(dcq + 1) * 256])))(),
             "d_y1c%d" % (j % 4), 16)

    for dc, tci, tc2, reuse in sched:
        pos_g = dc * 8 + tci
        if tci == 0 and dc >= 2:
            r.w("gpsimd", "s_down", (dc - 1) * 16)
            r.do("gpsimd",
                 (lambda dc=dc: (lambda e: e.dma_start(
                     wd_sb(dc % 2, 0, 16384),
                     wd_h[:, dc * 16384:(dc + 1) * 16384])))(),
                 "d_wd%d" % (dc % 2), 16)
        if not reuse:
            gl = len(gat_loads)
            par_g = gl % 2
            if gl < 2:
                r.w("sync", "d_gw0", 16 * 8 * 16)
                r.w("sync", "d_gw1", 16 * 8 * 16)
                r.w("sync", "sD1_g", 32)
            else:
                r.w("sync", "s_down", (pos_g - 1) * 2)
            r.do("sync",
                 (lambda tc2=tc2, par_g=par_g: (lambda e: e.dma_start(
                     AP(RB, par_g * 16384,
                        [[32768, 128], [256, 64], [1, 256]]),
                     AP(gat_hbm, tc2 * HTN * 32768,
                        [[256, 128], [32768, 64], [1, 256]]))))(),
                 "d_gr%d" % par_g, 16)
            gat_loads.append((par_g, gl // 2 + 1))
        par_g, grcnt = gat_loads[-1]
        for tsub in range(2):
            tt = tc2 * 2 + tsub
            pos = pos_g * 2 + tsub
            if pos + 3 < 128:
                j = pos + 3
                dcj = j // 32 if False else None
                dcq, r8 = divmod(j // 2, 8)
                tc2j = (list(range(8)) if dcq % 2 == 0 else list(range(7, -1, -1)))[r8]
                ttj = tc2j * 2 + (j % 2)
                if j == 3:
                    r.w("scalar", "d_gw0", 16 * 8 * 16)
                    r.w("scalar", "d_gw1", 16 * 8 * 16)
                r.w("scalar", "s_r2v", j - 3)
                r.do("scalar",
                     (lambda dcq=dcq, ttj=ttj, j=j: (lambda e: e.dma_start(
                         y1c_ap(j % 4),
                         y1f_hbm[ttj * 128:(ttj + 1) * 128,
                                 dcq * 256:(dcq + 1) * 256])))(),
                     "d_y1c%d" % (j % 4), 16)
            r.w("tensor", "d_wd%d" % (dc % 2), 16 * (dc // 2 + 1))
            r.w("tensor", "d_gr%d" % par_g, 16 * grcnt)
            r.w("tensor", "s_r2v", pos - 5)
            db = rot[pos % 6]
            for ht in range(HTN):
                r.do("tensor",
                     (lambda ht=ht, dc=dc, tsub=tsub, par_g=par_g, db=db:
                      (lambda e: e.matmul(
                          db,
                          gat_sb(par_g, ht * 256 + tsub * 128,
                                 ht * 256 + (tsub + 1) * 128),
                          wd_sb(dc % 2, ht * 256, (ht + 1) * 256),
                          start=(ht == 0), stop=(ht == HTN - 1))))(),
                     "s_down" if ht == HTN - 1 else None)
            r.w("vector", "s_down", pos + 1)
            r.w("vector", "d_y1c%d" % (pos % 4), 16 * (pos // 4 + 1))
            if dc < 7:
                if pos >= 4:
                    r.w("vector", "d_r2w%d" % (pos % 4), 16 * (pos // 4))
                r.do("vector",
                     (lambda pos=pos, db=db: (lambda e: e.tensor_tensor(
                         r2c_ap(pos % 4), db, y1c_ap(pos % 4),
                         ALU.add)))(),
                     "s_r2v", 1)
                r.w("scalar", "s_r2v", pos + 1)
                r.do("scalar",
                     (lambda dc=dc, tt=tt, pos=pos: (lambda e: e.dma_start(
                         r2_hbm[tt * 128:(tt + 1) * 128, dc * 256:(dc + 1) * 256],
                         r2c_ap(pos % 4))))(),
                     "d_r2w%d" % (pos % 4), 16)
            else:
                kf = pos - 112
                for pw in range(4):
                    r.w("scalar", "d_r2w%d" % pw, 16 * 28)
                if kf < 2:
                    r.w("scalar", "sD1_g", 32)
                r.w("scalar", "s_y2", kf - 1)
                r.do("scalar",
                     (lambda tt=tt: (lambda e: e.dma_start(
                         r2f_ap(tt % 2, 0, 1792),
                         r2_hbm[tt * 128:(tt + 1) * 128, :])))(),
                     "d_r2r%d" % (kf % 2), 16)
                r.do("vector",
                     (lambda tt=tt, db=db, pos=pos: (lambda e: e.tensor_tensor(
                         r2f_ap(tt % 2, 1792, 2048), db,
                         y1c_ap(pos % 4), ALU.add)))(),
                     "s_r2v", 1)
                r.w("scalar", "s_r2v", 112 + kf + 1)
                r.w("scalar", "d_r2r%d" % (kf % 2), 16 * (kf // 2 + 1))
                if kf >= 4:
                    r.w("scalar", "s_y2", kf - 3)
                r.do("scalar",
                     (lambda tt=tt: (lambda e: e.activation(
                         PTB[:, 0:2048], r2f_ap(tt % 2, 0, 2048), AF.Copy,
                         scale=RMEAN, accum_out=mI4(tt % 4))))(),
                     "s_stat2", 1)
                r.w("scalar", "s_stat2", r.c("s_stat2"))
                r.do("scalar",
                     (lambda tt=tt: (lambda e: e.activation(
                         PTB[:, 0:2048], r2f_ap(tt % 2, 0, 2048), AF.Square,
                         scale=RSQ, accum_out=e24(tt % 4))))(),
                     "s_stat2", 1)
                r.w("vector", "s_stat2", 2 * (kf + 1))
                r.do("vector",
                     (lambda tt=tt: (lambda e: e.scalar_tensor_tensor(
                         nv4(tt % 4), mI4(tt % 4), mI4(tt % 4), e24(tt % 4),
                         ALU.mult, ALU.subtract)))(),
                     "s_nvpe2", 1)
                r.w("scalar", "s_nvpe2", kf + 1)
                r.do("scalar",
                     (lambda tt=tt: (lambda e: e.activation(
                         sd4(tt % 4), nv4(tt % 4), AF.Sqrt, scale=-1.0,
                         bias=LNS[:, 20:21])))(),
                     "s_std2", 1)
                r.w("vector", "s_std2", kf + 1)
                r.do("vector",
                     (lambda tt=tt: (lambda e: e.reciprocal(rs4(tt % 4), sd4(tt % 4))))(),
                     "s_rst", 1)
                for _ in range(3):
                    r.w("vector", "s_jk", r.c("s_jk"))
                    r.do("vector", lambda e: e.tensor_copy(JK2, JK2), "s_jk", 1)
                if kf >= 2:
                    r.w("vector", "d_out%d" % (kf % 2), 16 * (kf // 2))
                r.w("vector", "s_rst", r.c("s_rst"))
                r.w("vector", "s_r2v", r.c("s_r2v"))
                r.do("vector",
                     (lambda tt=tt, kf=kf: (lambda e: e.tensor_scalar(
                         y2_ap(kf % 2), r2f_ap(tt % 2, 0, 2048), mI4(tt % 4),
                         rs4(tt % 4), ALU.subtract, ALU.mult)))(),
                     "s_y2", 1)
                r.w("scalar", "s_y2", kf + 1)
                r.do("scalar",
                     (lambda tt=tt, kf=kf: (lambda e: e.dma_start(
                         y_h[tt * 128:(tt + 1) * 128, :], y2_ap(kf % 2))))(),
                     "d_out%d" % (kf % 2), 16)

    r.w("sync", "d_out0", 16 * 8)
    r.w("sync", "d_out1", 16 * 8)


def tile_pm(w):
    R, C = w.shape
    n = R // 128
    return np.ascontiguousarray(w.reshape(n, 128, C).transpose(1, 0, 2).reshape(128, n * C))


def regroup(w128, n, k, m):
    return np.ascontiguousarray(
        w128.reshape(128, n, k, m).transpose(0, 2, 1, 3).reshape(128, n * k * m))


def make_core_inputs(x, wq, wk, wv, wo, wg, wu, wd):
    out = {}
    out["wq_h"] = tile_pm(wq)
    out["wk_h"] = tile_pm(wk)
    out["wv_h"] = tile_pm(wv)
    out["wo_h"] = regroup(tile_pm(wo), 16, 8, 256)
    wgT = np.ascontiguousarray(wg.T)
    wuT = np.ascontiguousarray(wu.T)
    wdT = np.ascontiguousarray(wd.T)
    out["wg_h"] = regroup(tile_pm(wgT), 16, 32, 256)
    out["wu_h"] = regroup(tile_pm(wuT), 16, 32, 256)
    out["wd_h"] = regroup(tile_pm(wdT), 64, 8, 256)
    out["id_h"] = np.eye(128, dtype=ml_dtypes.bfloat16)
    return out


def make_xt(xb, half):
    other = 1 - half
    xperm = np.concatenate([xb[other * 2048:(other + 1) * 2048],
                            xb[half * 2048:(half + 1) * 2048]], axis=0)
    xT = np.ascontiguousarray(xperm.T)
    return regroup(tile_pm(xT), 16, 16, 256)


def kernel(L0_attn_Q_input, L0_attn_Wq, L0_attn_Wk, L0_attn_Wv, L0_attn_Wo,
           L0_ffn_w_gate, L0_ffn_w_up, L0_ffn_w_down):
    from concourse.bass_utils import run_bass_kernel_spmd

    x = np.ascontiguousarray(np.asarray(L0_attn_Q_input, dtype=np.float32))
    wq = np.ascontiguousarray(np.asarray(L0_attn_Wq, dtype=np.float32))
    wk = np.ascontiguousarray(np.asarray(L0_attn_Wk, dtype=np.float32))
    wv = np.ascontiguousarray(np.asarray(L0_attn_Wv, dtype=np.float32))
    wo = np.ascontiguousarray(np.asarray(L0_attn_Wo, dtype=np.float32))
    wg = np.ascontiguousarray(np.asarray(L0_ffn_w_gate, dtype=np.float32))
    wu = np.ascontiguousarray(np.asarray(L0_ffn_w_up, dtype=np.float32))
    wd = np.ascontiguousarray(np.asarray(L0_ffn_w_down, dtype=np.float32))

    shared = make_core_inputs(x, wq, wk, wv, wo, wg, wu, wd)
    in_maps = []
    for core in range(8):
        b, half = core // 2, core % 2
        m = dict(shared)
        m["xt_h"] = make_xt(x[b], half)
        m["xn_h"] = np.ascontiguousarray(x[b, half * 2048:(half + 1) * 2048])
        in_maps.append(m)

    nc = build(upto="full", debug=False)
    res = run_bass_kernel_spmd(nc, in_maps, core_ids=list(range(8)))

    out = np.empty((4, 4096, 2048), dtype=np.float32)
    for core in range(8):
        b, half = core // 2, core % 2
        out[b, half * 2048:(half + 1) * 2048] = np.asarray(
            res.results[core]["y_h"], dtype=np.float32)
    return out
